# version 17
# baseline (speedup 1.0000x reference)
import numpy as np
from contextlib import ExitStack

import concourse.bass as bass
import concourse.mybir as mybir
from concourse.bass_utils import run_bass_kernel_spmd
from concourse.ap import AP

F32 = mybir.dt.float32
AF = mybir.ActivationFunctionType
ALU = mybir.AluOpType
AX = mybir.AxisListType

S = 2048
DM = 1024
NPAIR = 8
NT = 16
NES = 128
TG = 32
FF = 32
NG = 64
QW = 2175
SCRW = 49152

DO_FFN1 = True
DO_ATTN = True
DO_FFN2 = True
DBG = ""


class Bld:
    ENGS = ("sync", "gpsimd", "tensor", "vector", "scalar")

    def __init__(self, nc, stack):
        self.nc = nc
        self.stack = stack
        self.th = {e: [] for e in self.ENGS}
        self.sems = {}
        self.cnt = {}

    def sem(self, name):
        self.sems[name] = self.stack.enter_context(self.nc.semaphore(name))
        self.cnt[name] = 0

    def wait(self, eng, semname, val):
        if val is None or val <= 0:
            return
        s = self.sems[semname]
        self.th[eng].append(lambda e, s=s, v=val: e.wait_ge(s, v))

    def op(self, eng, fn, inc=None, n=1):
        if inc is None:
            self.th[eng].append(fn)
            return None
        self.cnt[inc] += n
        s = self.sems[inc]
        self.th[eng].append(lambda e, fn=fn, s=s, n=n: fn(e).then_inc(s, n))
        return self.cnt[inc]

    def dma(self, eng, out, in_, inc):
        return self.op(eng, lambda e, o=out, i=in_: e.dma_start(out=o, in_=i),
                       inc=inc, n=16)


def build_nc():
    nc = bass.Bass("TRN2", target_bir_lowering=False)

    x = nc.dram_tensor("x", [S, DM], F32, kind="ExternalInput").ap()
    wqt = nc.dram_tensor("wqt", [DM, DM], F32, kind="ExternalInput").ap()
    wkt = nc.dram_tensor("wkt", [DM, DM], F32, kind="ExternalInput").ap()
    wvt = nc.dram_tensor("wvt", [DM, DM], F32, kind="ExternalInput").ap()
    wot = nc.dram_tensor("wot", [DM, DM], F32, kind="ExternalInput").ap()
    bqs = nc.dram_tensor("bqs", [128, 8], F32, kind="ExternalInput").ap()
    bks = nc.dram_tensor("bks", [128, 8], F32, kind="ExternalInput").ap()
    bvrep = nc.dram_tensor("bvrep", [128, DM], F32, kind="ExternalInput").ap()
    borep = nc.dram_tensor("borep", [128, DM], F32, kind="ExternalInput").ap()
    gttd = nc.dram_tensor("gttd", [128, 4095], F32, kind="ExternalInput").ap()
    ctrlc = nc.dram_tensor("ctrlc", [128, 1024], F32, kind="ExternalInput").ap()
    f1r = nc.dram_tensor("f1r", [DM, NES * FF], F32, kind="ExternalInput").ap()
    f1b = nc.dram_tensor("f1b", [1, NES * FF], F32, kind="ExternalInput").ap()
    f1brep = nc.dram_tensor("f1brep", [32, NES * FF], F32, kind="ExternalInput").ap()
    f2r = nc.dram_tensor("f2r", [NES * FF, DM], F32, kind="ExternalInput").ap()
    identh = nc.dram_tensor("identh", [128, 128], F32, kind="ExternalInput").ap()
    tie4h = nc.dram_tensor("tie4h", [128, 128], F32, kind="ExternalInput").ap()

    out = nc.dram_tensor("out", [S, DM], F32, kind="ExternalOutput").ap()

    xs1 = nc.dram_tensor("xs1", [S, DM], F32).ap()
    xs2 = nc.dram_tensor("xs2", [S, DM], F32).ap()
    aofm = nc.dram_tensor("aofm", [DM, S], F32).ap()
    qsl = nc.dram_tensor("qsl", [3 * 128, QW], F32).ap()
    itb = nc.dram_tensor("itb", [NES * NG, DM], F32).ap()
    hsc = nc.dram_tensor("hsc", [S, DM], F32).ap()

    with ExitStack() as stack:
        SCR = stack.enter_context(nc.sbuf_tensor("SCR", [128, SCRW], F32))
        IDENT = stack.enter_context(nc.sbuf_tensor("IDENT", [128, 128], F32))
        TIE4 = stack.enter_context(nc.sbuf_tensor("TIE4", [128, 128], F32))
        EPS = stack.enter_context(nc.sbuf_tensor("EPS", [128, 1], F32))
        P0 = stack.enter_context(nc.psum_tensor("P0", [128, 2560], F32))
        P1 = stack.enter_context(nc.psum_tensor("P1", [128, 1024], F32))
        P2 = stack.enter_context(nc.psum_tensor("P2", [128, 512], F32))

        b = Bld(nc, stack)
        for s_ in ("ld", "sx", "so", "fla", "flb", "fba", "fbb", "fbc", "fbd",
                   "gm", "qwa", "qwb", "qwc", "gra", "grb", "grc",
                   "iwa", "iwb", "rha", "rhb", "xta", "xtb", "aow",
                   "hw", "hr", "lxa", "lxb", "lxc", "lxd",
                   "rga", "rgb", "rgc", "rgd", "rhc", "rhd",
                   "pe", "ac", "ve"):
            b.sem(s_)
        DMA_SEMS = ("ld", "sx", "so", "fla", "flb", "fba", "fbb", "fbc", "fbd",
                    "qwa", "qwb", "qwc",
                    "gra", "grb", "grc", "iwa", "iwb", "rha", "rhb",
                    "xta", "xtb", "aow", "hw", "hr", "lxa", "lxb", "lxc", "lxd",
                    "rga", "rgb", "rgc", "rgd", "rhc", "rhd")

        def barrier(eng):
            for s_ in DMA_SEMS + ("gm", "pe", "ac", "ve"):
                b.wait(eng, s_, b.cnt[s_])

        def scr(lo, hi, p0=0, p1=128):
            return SCR[p0:p1, lo:hi]

        GTT = 0
        CT = 4096
        BVR = 5120
        BOR = 6144
        BQ = 7168
        BK = 7176
        STT = 7184
        ONES = 7264
        SMALL = 7280
        XA = 7296
        HBASE = XA + 16384
        TAIL = HBASE + 16384
        assert TAIL + 8448 <= SCRW

        b.dma("sync", SCR[:, GTT:GTT + 4095], gttd[:, :], "ld")
        b.dma("sync", SCR[:, CT:CT + 1024], ctrlc[:, :], "ld")
        b.dma("sync", SCR[:, BVR:BVR + 1024], bvrep[:, :], "ld")
        b.dma("sync", SCR[:, BOR:BOR + 1024], borep[:, :], "ld")
        b.dma("sync", SCR[:, BQ:BQ + 8], bqs[:, :], "ld")
        b.dma("sync", SCR[:, BK:BK + 8], bks[:, :], "ld")
        b.dma("sync", IDENT[:, :], identh[:, :], "ld")
        b.dma("sync", TIE4[:, :], tie4h[:, :], "ld")
        consts_ld = b.cnt["ld"]
        b.op("gpsimd", lambda e: e.memset(EPS[:, :], 1e-5), inc="gm", n=1)
        b.op("gpsimd", lambda e: e.memset(scr(ONES, ONES + 64, 0, 1), 1.0),
             inc="gm", n=1)
        consts_gp = b.cnt["gm"]

        def emit_ln(src_hbm, mode):
            barrier("sync")
            LXS = ("lxa", "lxb", "lxc", "lxd")
            lx_ld = []
            for ch in range(4):
                src_ap = AP(src_hbm.tensor, src_hbm.offset + 512 * DM * ch,
                            [[DM, 128], [128 * DM, 4], [1, DM]])
                lx_ld.append(b.dma("sync",
                                   AP(SCR[:, :].tensor, XA + 4096 * ch,
                                      [[SCRW, 128], [DM, 4], [1, DM]]),
                                   src_ap, LXS[ch]))
            vsum = 0
            for ch in range(4):
                b.wait("vector", LXS[ch], lx_ld[ch])
                vsum = b.op("vector",
                            lambda e, ch=ch: e.reduce_sum(
                                scr(STT + 4 * ch, STT + 4 * ch + 4),
                                AP(SCR[:, :].tensor, XA + 4096 * ch,
                                   [[SCRW, 128], [DM, 4], [1, DM]]),
                                axis=AX.X),
                            inc="ve")
            for t in range(NT):
                b.wait("scalar", LXS[t // 4], lx_ld[t // 4])
                b.op("scalar",
                     lambda e, t=t: e.activation(
                         SCR[:, HBASE + 1024 * t:HBASE + 1024 * (t + 1)],
                         scr(XA + 1024 * t, XA + 1024 * (t + 1)),
                         AF.Square, accum_out=scr(STT + 16 + t, STT + 17 + t)),
                     inc="ac")
            acs = b.cnt["ac"]
            b.wait("vector", "ve", vsum)
            vmean = b.op("vector",
                         lambda e: e.tensor_scalar(
                             scr(STT + 32, STT + 48), scr(STT, STT + 16),
                             1.0 / DM, None, ALU.mult),
                         inc="ve")
            b.wait("vector", "ve", vmean)
            vmsq = b.op("vector",
                        lambda e: e.tensor_tensor(
                            scr(SMALL, SMALL + 16), scr(STT + 32, STT + 48),
                            scr(STT + 32, STT + 48), ALU.mult),
                        inc="ve")
            b.wait("vector", "ac", acs)
            b.wait("vector", "ve", vmsq)
            vvar = b.op("vector",
                        lambda e: e.scalar_tensor_tensor(
                            scr(STT + 48, STT + 64), scr(STT + 16, STT + 32),
                            1.0 / DM, scr(SMALL, SMALL + 16),
                            ALU.mult, ALU.subtract),
                        inc="ve")
            b.wait("scalar", "ve", vvar)
            b.wait("scalar", "gm", consts_gp)
            a1 = b.op("scalar",
                      lambda e: e.activation(scr(SMALL, SMALL + 16),
                                             scr(STT + 48, STT + 64),
                                             AF.Ln, bias=EPS[:, 0:1], scale=1.0),
                      inc="ac")
            b.wait("scalar", "ac", a1)
            a2 = b.op("scalar",
                      lambda e: e.activation(scr(STT + 64, STT + 80),
                                             scr(SMALL, SMALL + 16),
                                             AF.Exp, scale=-0.5),
                      inc="ac")
            b.wait("vector", "ac", a2)
            if mode == "tok":
                last = None
                for t in range(NT):
                    last = b.op("vector",
                                lambda e, t=t: e.tensor_scalar(
                                    scr(HBASE + 1024 * t, HBASE + 1024 * (t + 1)),
                                    scr(XA + 1024 * t, XA + 1024 * (t + 1)),
                                    scr(STT + 32 + t, STT + 33 + t),
                                    scr(STT + 64 + t, STT + 65 + t),
                                    ALU.subtract, ALU.mult),
                                inc="ve")
                    if t >= 8:
                        b.wait("sync", "ve", last)
                        b.dma("sync", hsc[128 * t:128 * (t + 1), :],
                              SCR[:, HBASE + 1024 * t:HBASE + 1024 * (t + 1)],
                              "hw")
                return last
            for t in range(NT):
                ht = b.op("vector",
                          lambda e, t=t: e.tensor_scalar(
                              scr(XA + 1024 * t, XA + 1024 * (t + 1)),
                              scr(XA + 1024 * t, XA + 1024 * (t + 1)),
                              scr(STT + 32 + t, STT + 33 + t),
                              scr(STT + 64 + t, STT + 65 + t),
                              ALU.subtract, ALU.mult),
                          inc="ve")
                b.wait("tensor", "ve", ht)
                b.wait("tensor", "ve", b.cnt["ve"])
                for m in range(8):
                    b.op("tensor",
                         lambda e, t=t, m=m: e.matmul(
                             P1[:, 128 * m:128 * (m + 1)],
                             scr(XA + 1024 * t + 128 * m,
                                 XA + 1024 * t + 128 * (m + 1)),
                             IDENT[:, :], start=True, stop=True,
                             is_transpose=True),
                         inc="pe")
                pev = b.cnt["pe"]
                b.wait("vector", "pe", pev)
                b.op("vector",
                     lambda e, t=t: e.tensor_copy(
                         AP(SCR[:, :].tensor, HBASE + 128 * t,
                            [[SCRW, 128], [2048, 8], [1, 128]]),
                         P1[:, 0:1024]),
                     inc="ve")
            return b.cnt["ve"]

        def emit_ffn(src_hbm, dst_hbm, h_ve):
            PRMP = XA + 0
            HFM = XA + 8192
            STG = XA + 8192
            INT = XA + 9216
            ITS = XA + 9728
            F1BB = XA + 10240
            F1BUF = XA + 10368
            F2BUF = XA + 11392
            F2B = XA + 14464
            ITS2 = XA + 12416
            XT = XA + 10240
            DS = HBASE + 8192
            RHS2 = TAIL

            b.wait("vector", "ve", h_ve)
            b.wait("tensor", "ve", h_ve)
            b.wait("scalar", "ve", h_ve)

            vz = b.op("vector",
                      lambda e: e.memset(scr(PRMP, PRMP + 8192), 0.0),
                      inc="ve")

            for t in range(NT):
                b.wait("tensor", "ve", b.cnt["ve"])
                for m in range(8):
                    b.op("tensor",
                         lambda e, t=t, m=m: e.matmul(
                             P1[:, 128 * m:128 * (m + 1)],
                             scr(HBASE + 1024 * t + 128 * m,
                                 HBASE + 1024 * t + 128 * (m + 1)),
                             IDENT[:, :], start=True, stop=True,
                             is_transpose=True),
                         inc="pe")
                pet = b.cnt["pe"]
                b.wait("vector", "pe", pet)
                vh = b.op("vector",
                          lambda e: e.tensor_copy(scr(HFM, HFM + 1024),
                                                  P1[:, 0:1024]),
                          inc="ve")
                b.wait("tensor", "ve", vh)
                for m in range(8):
                    b.op("tensor",
                         lambda e, m=m: e.matmul(
                             P2[:, 0:128],
                             scr(CT + 128 * m, CT + 128 * (m + 1)),
                             scr(HFM + 128 * m, HFM + 128 * (m + 1)),
                             start=(m == 0), stop=(m == 7)),
                         inc="pe")
                pel = b.cnt["pe"]
                b.wait("vector", "pe", pel)
                v1 = b.op("vector",
                          lambda e: e.tensor_tensor(scr(SMALL + 4, SMALL + 4 + 128)
                                                    if False else
                                                    scr(STG, STG + 128),
                                                    P2[:, 0:128], TIE4[:, :],
                                                    ALU.add),
                          inc="ve")
                b.wait("vector", "ve", v1)
                v2 = b.op("vector",
                          lambda e: e.reduce_max(
                              scr(SMALL, SMALL + 4),
                              AP(SCR[:, :].tensor, STG,
                                 [[SCRW, 128], [32, 4], [1, 32]]),
                              axis=AX.X),
                          inc="ve")
                b.wait("vector", "ve", v2)
                if t == 0:
                    b.wait("vector", "ve", vz)
                for a in range(4):
                    b.op("vector",
                         lambda e, a=a, t=t: e.tensor_scalar(
                             scr(PRMP + 512 * t + 160 * a,
                                 PRMP + 512 * t + 160 * a + 32),
                             scr(STG + 32 * a, STG + 32 * (a + 1)),
                             scr(SMALL + a, SMALL + a + 1), None, ALU.is_equal),
                         inc="ve")
            vperm_done = b.cnt["ve"]
            if DBG == "ffa":
                barrier("sync")
                for t in range(8):
                    b.dma("sync", out[128 * t:128 * (t + 1), :],
                          SCR[:, PRMP + 1024 * t:PRMP + 1024 * (t + 1)], "so")
                for t in range(2):
                    b.dma("sync", out[128 * (8 + t):128 * (9 + t), :],
                          SCR[:, PRT + 1024 * t:PRT + 1024 * (t + 1)], "so")
                b.dma("sync", out[128 * 10:128 * 11, :],
                      SCR[:, STG:STG + 1024], "so")
                b.dma("sync", out[128 * 11:128 * 12, :],
                      SCR[:, HBASE + 1024 * 15:HBASE + 1024 * 16], "so")
                b.dma("sync", out[128 * 12:128 * 13, :],
                      SCR[:, HFM:HFM + 1024], "so")
                b.dma("sync", out[128 * 14:128 * 15, :],
                      SCR[:, CT:CT + 1024], "so")
                b.dma("sync", out[128 * 15:128 * 16, :],
                      SCR[:, XA + 1024 * 15:XA + 1024 * 16], "so")
                b.wait("sync", "so", b.cnt["so"])
                return None

            b.wait("tensor", "ve", vperm_done)
            b.wait("vector", "ve", vperm_done)
            zpad = 0
            for (lo, hi) in ((STG, STG + 1024), (INT, INT + 512),
                             (F2BUF, F2BUF + 1024), (F2B, F2B + 1024),
                             (GTT + 1152, GTT + 2176), (GTT + 2176, GTT + 3200)):
                for (p0, p1) in ((32, 64), (64, 128)):
                    zpad = b.op("vector",
                                lambda e, lo=lo, hi=hi, p0=p0, p1=p1: e.memset(
                                    scr(lo, hi, p0, p1), 0.0),
                                inc="ve", n=1)
            b.wait("tensor", "ve", zpad)

            vf1_prev = 0
            vT_prev = 0
            pef1_prev = 0
            vb_prev = 0
            gp_prev_it = [0, 0]
            dve = b.cnt["ve"]
            F2S = (F2BUF, F2B, GTT + 1152, GTT + 2176)
            F2SEM = ("fba", "fbb", "fbc", "fbd")
            es_seq = [64 * grp_ + 4 * c_ + e4_
                      for gh_ in range(2) for grp_ in range(2)
                      for c_ in range(16) for e4_ in range(4)]
            f2_rd = [0, 0, 0, 0]
            f2_ld = [0, 0, 0, 0]
            ac_hist = [0] * (len(es_seq) + 2)
            f2_idx = 0

            def load_f2(j):
                if j >= len(es_seq):
                    return
                slot = j % 4
                b.wait("sync", "pe", f2_rd[slot])
                f2_ld[slot] = b.dma(
                    "sync", scr(F2S[slot], F2S[slot] + 1024, 0, 32),
                    f2r[32 * es_seq[j]:32 * es_seq[j] + 32, :], F2SEM[slot])

            F1SL = (F1BUF, GTT)
            F1BBSL = (F1BB, GTT + 1024)
            F1SEM = ("fla", "flb")
            es0_seq = [64 * grp_ + 4 * c_
                       for gh_ in range(2) for grp_ in range(2)
                       for c_ in range(16)]
            NPK = len(es0_seq)
            pef1_hist = [0] * NPK
            vb_hist = [0] * NPK
            f1_ld = [0, 0]
            pk = 0

            def load_f1(j):
                if j >= NPK:
                    return
                slot = j % 2
                if j >= 2:
                    b.wait("sync", "pe", pef1_hist[j - 2])
                    b.wait("sync", "ve", vb_hist[j - 2])
                es0_ = es0_seq[j]
                b.dma("sync",
                      AP(SCR[:, :].tensor, F1SL[slot],
                         [[SCRW, 128], [128, 8], [1, 128]]),
                      AP(f1r.tensor, 32 * es0_,
                         [[4096, 128], [4096 * 128, 8], [1, 128]]),
                      F1SEM[slot])
                f1_ld[slot] = b.dma(
                    "sync", scr(F1BBSL[slot], F1BBSL[slot] + 128, 0, 32),
                    f1brep[0:32, 32 * es0_:32 * es0_ + 128], F1SEM[slot])

            b.wait("sync", "ve", h_ve)
            load_f2(0)
            load_f2(1)
            load_f2(2)
            load_f2(3)
            load_f1(0)
            load_f1(1)
            hr_ld = [0] * 8
            b.wait("vector", "hw", b.cnt["hw"])
            for gh in range(2):
                for grp in range(2):
                    for ti in range(8):
                        t = 8 * gh + ti
                        hb = HBASE + 1024 * ti
                        if gh == 1 and ti == 0 and grp == 0:
                            b.wait("tensor", "hr", hr_ld[7])
                        b.wait("tensor", "ve", max(dve, vf1_prev, vT_prev))
                        if ti == 0:
                            b.wait("tensor", "ac", b.cnt["ac"])
                        for a in range(4):
                            b.op("tensor",
                                 lambda e, t=t, a=a: e.matmul(
                                     P2[:, 128 * a:128 * (a + 1)],
                                     scr(PRMP + 512 * t + 128 * a,
                                         PRMP + 512 * t + 128 * (a + 1)),
                                     IDENT[:, :], start=(a == 0), stop=(a == 3),
                                     is_transpose=True),
                                 inc="pe")
                        pez = b.cnt["pe"]
                        b.wait("vector", "pe", pez)
                        vz2 = b.op("vector",
                                   lambda e: e.tensor_copy(scr(ITS, ITS + 512),
                                                           P2[:, 0:512]),
                                   inc="ve")
                        b.wait("tensor", "ve", vz2)
                        for m in range(8):
                            for a in range(4):
                                b.op("tensor",
                                     lambda e, hb=hb, m=m, a=a, grp=grp: e.matmul(
                                         P0[:, 256 * m + 64 * a:
                                            256 * m + 64 * (a + 1)],
                                         scr(hb + 128 * m, hb + 128 * (m + 1)),
                                         scr(ITS + 128 * a + 64 * grp,
                                             ITS + 128 * a + 64 * grp + 64),
                                         start=(a == 0 and (m % 2) == 0),
                                         stop=(a == 3 and (m % 2) == 1)),
                                     inc="pe")
                            if m % 2 == 1:
                                bk2 = m // 2
                                ped = b.cnt["pe"]
                                b.wait("vector", "pe", ped)
                                dve = b.op("vector",
                                           lambda e, bk2=bk2, ti=ti: e.tensor_copy(
                                               AP(SCR[:, :].tensor,
                                                  DS + 4096 * bk2 + 4 * ti,
                                                  [[SCRW, 128], [2048, 2],
                                                   [1, 4], [32, 64]]),
                                               AP(P0[:, :].tensor,
                                                  P0[:, :].offset + 512 * bk2,
                                                  [[2560, 128], [256, 2],
                                                   [64, 4], [1, 64]])),
                                           inc="ve")
                    if gh == 0 and grp == 1:
                        b.wait("sync", "pe", b.cnt["pe"])
                        b.wait("sync", "hw", b.cnt["hw"])
                        for k in range(8):
                            hr_ld[k] = b.dma(
                                "sync",
                                SCR[:, HBASE + 1024 * k:HBASE + 1024 * (k + 1)],
                                hsc[128 * (8 + k):128 * (9 + k), :], "hr")
                    b.wait("tensor", "ve", dve)

                    def emit_f1(c):
                        nonlocal pk, vb_prev, vf1_prev
                        es0 = 64 * grp + 4 * c
                        cb = c % 8
                        f1slot = pk % 2
                        f1buf = F1SL[f1slot]
                        f1bb = F1BBSL[f1slot]
                        b.wait("tensor", F1SEM[f1slot], f1_ld[f1slot])
                        b.wait("tensor", "ve", vb_prev)
                        f1cols = []
                        for e4 in range(4):
                            es_loc = 4 * c + e4
                            col = 512 * (e4 % 2) + 32 * (e4 // 2)
                            f1cols.append(col)
                            for m in range(8):
                                b.op("tensor",
                                     lambda e, m=m, es_loc=es_loc, col=col,
                                     e4=e4, f1buf=f1buf: e.matmul(
                                         P1[0:32, col:col + 32],
                                         AP(SCR[:, :].tensor,
                                            DS + 2048 * m + 32 * es_loc,
                                            [[SCRW, 128], [1, 32]]),
                                         scr(f1buf + 128 * m + 32 * e4,
                                             f1buf + 128 * m + 32 * (e4 + 1)),
                                         start=(m == 0 and e4 < 2),
                                         stop=(m == 7 and e4 >= 2)),
                                     inc="pe")
                        pef1 = b.cnt["pe"]
                        pef1_hist[pk] = pef1
                        return dict(cb=cb, es0=es0, pef1=pef1,
                                    f1cols=f1cols, f1bb=f1bb,
                                    f1slot=f1slot, pk=pk)

                    def emit_bias(st):
                        nonlocal pk, vb_prev, vf1_prev
                        b.wait("vector", "pe", st["pef1"])
                        b.wait("vector", F1SEM[st["f1slot"]],
                               f1_ld[st["f1slot"]])
                        cb = st["cb"]
                        for e4 in range(4):
                            b.op("vector",
                                 lambda e, e4=e4, c=cb, col=st["f1cols"][e4],
                                 f1bb=st["f1bb"]: e.tensor_tensor(
                                     SCR[0:32, STG + 128 * c + 32 * e4:
                                         STG + 128 * c + 32 * (e4 + 1)],
                                     P1[0:32, col:col + 32],
                                     scr(f1bb + 32 * e4, f1bb + 32 * (e4 + 1),
                                         0, 32),
                                     ALU.add),
                                 inc="ve")
                        vb = b.cnt["ve"]
                        vb_hist[st["pk"]] = vb
                        vb_prev = vb
                        b.wait("vector", "ve", vb)
                        st["vf1"] = b.op("vector",
                                         lambda e, c=cb: e.tensor_scalar(
                                             SCR[0:32, STG + 128 * c:
                                                 STG + 128 * (c + 1)],
                                             SCR[0:32, STG + 128 * c:
                                                 STG + 128 * (c + 1)],
                                             0.0, None, ALU.max),
                                         inc="ve")
                        vf1_prev = st["vf1"]
                        load_f1(st["pk"] + 2)
                        pk = st["pk"] + 1

                    def emit_tail(st):
                        nonlocal vT_prev, f2_idx
                        cb = st["cb"]
                        b.wait("tensor", "ve", st["vf1"])
                        b.wait("tensor", "ve", vT_prev)
                        for e4 in range(4):
                            b.op("tensor",
                                 lambda e, e4=e4, c=cb: e.matmul(
                                     P2[0:32, 128 * e4:128 * (e4 + 1)],
                                     SCR[0:128, STG + 128 * c + 32 * e4:
                                         STG + 128 * c + 32 * (e4 + 1)],
                                     IDENT[:, :], start=(e4 == 0),
                                     stop=(e4 == 3),
                                     is_transpose=True),
                                 inc="pe")
                        peT = b.cnt["pe"]
                        b.wait("vector", "pe", peT)
                        vT = b.op("vector",
                                  lambda e: e.tensor_copy(
                                      SCR[0:32, INT:INT + 512],
                                      P2[0:32, 0:512]),
                                  inc="ve")
                        vT_prev = vT
                        for e4 in range(4):
                            es = st["es0"] + e4
                            slot = f2_idx % 4
                            its2 = f2_idx % 2
                            iw = "iwa" if its2 == 0 else "iwb"
                            pb = 1024 * its2
                            b.wait("tensor", F2SEM[slot], f2_ld[slot])
                            if e4 == 0:
                                b.wait("tensor", "ve", vT)
                            if f2_idx >= 2:
                                b.wait("tensor", "ac", ac_hist[f2_idx - 2])
                            for bk in range(2):
                                b.op("tensor",
                                     lambda e, e4=e4, bk=bk, pb=pb,
                                     fo=F2S[slot]: e.matmul(
                                         P0[0:32, pb + 512 * bk:
                                            pb + 512 * (bk + 1)],
                                         SCR[0:128, INT + 128 * e4:
                                             INT + 128 * e4 + 32],
                                         scr(fo + 512 * bk,
                                             fo + 512 * (bk + 1), 0, 128),
                                         start=True, stop=True),
                                     inc="pe")
                            pef2 = b.cnt["pe"]
                            f2_rd[slot] = pef2
                            load_f2(f2_idx + 4)
                            b.wait("scalar", "pe", pef2)
                            b.wait("scalar", iw, gp_prev_it[its2])
                            acv = b.op("scalar",
                                       lambda e, its2=its2, pb=pb: e.activation(
                                           scr(ITS2 + 1024 * its2,
                                               ITS2 + 1024 * its2 + 1024,
                                               0, 32),
                                           P0[0:32, pb:pb + 1024], AF.Copy,
                                           scale=1.0),
                                       inc="ac")
                            ac_hist[f2_idx] = acv
                            b.wait("gpsimd", "ac", acv)
                            gp_prev_it[its2] = b.dma(
                                "gpsimd",
                                itb[64 * es + 32 * gh:
                                    64 * es + 32 * gh + 32, :],
                                scr(ITS2 + 1024 * its2,
                                    ITS2 + 1024 * its2 + 1024, 0, 32),
                                iw)
                            f2_idx += 1

                    prev_st = None
                    for c in range(16):
                        st = emit_f1(c)
                        if prev_st is not None:
                            emit_tail(prev_st)
                        emit_bias(st)
                        prev_st = st
                    emit_tail(prev_st)
            if DBG == "ffb2":
                barrier("sync")
                return None

            b.wait("sync", "iwa", b.cnt["iwa"])
            b.wait("sync", "iwb", b.cnt["iwb"])
            b.wait("sync", "pe", b.cnt["pe"])
            b.wait("sync", "ve", b.cnt["ve"])
            b.wait("gpsimd", "pe", b.cnt["pe"])
            b.wait("gpsimd", "ve", b.cnt["ve"])
            b.wait("tensor", "ve", b.cnt["ve"])

            RHSS = (RHS2, RHS2 + 4096, DS, DS + 4096)
            RSEM_S = ("rha", "rhb", "rhc", "rhd")
            RSEM_G = ("rga", "rgb", "rgc", "rgd")

            def load_rhs(t):
                slot = t % 4
                lasts = [0, 0]
                for a in range(4):
                    g = 4 * t + a
                    src = AP(itb.tensor, g * DM, [[NG * DM, 128], [1, DM]])
                    eng = "sync" if a % 2 == 0 else "gpsimd"
                    sem = RSEM_S[slot] if a % 2 == 0 else RSEM_G[slot]
                    lasts[a % 2] = b.dma(eng,
                                         SCR[:, RHSS[slot] + 1024 * a:
                                             RHSS[slot] + 1024 * (a + 1)],
                                         src, sem)
                return lasts

            def load_xt(t, slot):
                return b.dma("sync",
                             SCR[:, XT + 1024 * slot:XT + 1024 * (slot + 1)],
                             src_hbm[128 * t:128 * (t + 1), :],
                             "xta" if slot == 0 else "xtb")

            pre_rhs = [load_rhs(0), load_rhs(1), load_rhs(2)]
            pre_xt = [load_xt(0, 0)]
            pesc_hist = [0] * NT
            vres_hist = [0] * NT
            st_hist = [0] * NT
            st_last = None
            for t in range(NT):
                if t + 3 < NT:
                    if t >= 1:
                        b.wait("sync", "pe", pesc_hist[t - 1])
                        b.wait("gpsimd", "pe", pesc_hist[t - 1])
                    pre_rhs.append(load_rhs(t + 3))
                if t + 1 < NT:
                    if t >= 1:
                        b.wait("sync", "so", st_hist[t - 1])
                    pre_xt.append(load_xt(t + 1, (t + 1) % 2))
                b.wait("tensor", RSEM_S[t % 4], pre_rhs[t][0])
                b.wait("tensor", RSEM_G[t % 4], pre_rhs[t][1])
                if t >= 1:
                    b.wait("tensor", "ve", vres_hist[t - 1])
                for bk in range(2):
                    for a in range(4):
                        b.op("tensor",
                             lambda e, t=t, a=a, bk=bk: e.matmul(
                                 P1[:, 512 * bk:512 * (bk + 1)],
                                 scr(PRMP + 512 * t + 128 * a,
                                     PRMP + 512 * t + 128 * (a + 1)),
                                 SCR[:, RHSS[t % 4] + 1024 * a + 512 * bk:
                                     RHSS[t % 4] + 1024 * a
                                     + 512 * (bk + 1)],
                                 start=(a == 0), stop=(a == 3)),
                             inc="pe")
                pesc_hist[t] = b.cnt["pe"]
                b.wait("vector", "pe", pesc_hist[t])
                b.wait("vector", "xta" if t % 2 == 0 else "xtb", pre_xt[t])
                vres_hist[t] = b.op("vector",
                                    lambda e, t=t: e.scalar_tensor_tensor(
                                        SCR[:, XT + 1024 * (t % 2):
                                            XT + 1024 * (t % 2) + 1024],
                                        P1[:, 0:1024], 0.5,
                                        SCR[:, XT + 1024 * (t % 2):
                                            XT + 1024 * (t % 2) + 1024],
                                        ALU.mult, ALU.add),
                                    inc="ve")
                b.wait("sync", "ve", vres_hist[t])
                st_last = b.dma("sync", dst_hbm[128 * t:128 * (t + 1), :],
                                SCR[:, XT + 1024 * (t % 2):
                                    XT + 1024 * (t % 2) + 1024], "so")
                st_hist[t] = st_last
            return st_last

        def emit_attn(src_hbm, dst_hbm, hfm_ve):
            QF = XA + 0
            KF = XA + 4096
            VS = XA + 8192
            ET = XA + 10304
            SQR0 = XA + 12352
            SQR1 = TAIL + 6144
            WSL = TAIL
            SPOS = TAIL
            AOT = XA + 14528
            AOV = XA + 14656
            PVT = XA + 14720

            def sqr_off(i):
                return SQR0 if i % 2 == 0 else SQR1

            b.wait("tensor", "ve", hfm_ve)
            b.wait("scalar", "ve", hfm_ve)
            b.wait("gpsimd", "ve", hfm_ve)

            gtt_ld = b.dma("sync", SCR[:, GTT:GTT + 4095], gttd[:, :], "ld")
            b.wait("tensor", "ld", gtt_ld)

            b.op("gpsimd",
                 lambda e: e.memset(AP(SCR[:, :].tensor, VS + 64,
                                       [[SCRW, 128], [130, 16]]), 1.0),
                 inc="gm", n=1)
            b.op("gpsimd",
                 lambda e: e.memset(AP(SCR[:, :].tensor, VS + 129,
                                       [[SCRW, 128], [130, 16]]), 1.0),
                 inc="gm", n=1)
            gp_ones = b.cnt["gm"]

            NIT = 32
            DEPTH = 3

            for p in range(NPAIR):
                b.wait("sync", "pe", b.cnt["pe"])
                wl = None
                for (wt, woff) in ((wqt, 0), (wkt, 1), (wvt, 2)):
                    src = AP(wt.tensor, 128 * p, [[DM, 128], [128 * DM, 8],
                                                  [1, 128]])
                    wl = b.dma("sync",
                               AP(SCR[:, :].tensor, WSL + 1024 * woff,
                                  [[SCRW, 128], [128, 8], [1, 128]]),
                               src, "sx")
                b.wait("tensor", "sx", wl)
                b.wait("tensor", "ac", b.cnt["ac"])
                b.wait("tensor", "ve", b.cnt["ve"])

                aq_nb = [0] * 4
                for (dst, wof, bof) in ((QF, 0, BQ), (KF, 1, BK)):
                    for hh in range(2):
                        for nb in range(4):
                            b.wait("tensor", "ac", aq_nb[nb])
                            for m in range(8):
                                b.op("tensor",
                                     lambda e, wof=wof, hh=hh, nb=nb, m=m: e.matmul(
                                         P0[0:64, 512 * nb:512 * (nb + 1)],
                                         scr(WSL + 1024 * wof + 128 * m + 64 * hh,
                                             WSL + 1024 * wof + 128 * m
                                             + 64 * (hh + 1)),
                                         scr(HBASE + 2048 * m + 512 * nb,
                                             HBASE + 2048 * m + 512 * (nb + 1)),
                                         start=(m == 0), stop=(m == 7)),
                                     inc="pe")
                            peq = b.cnt["pe"]
                            b.wait("scalar", "pe", peq)
                            aq_nb[nb] = b.op(
                                "scalar",
                                lambda e, dst=dst, bof=bof, p=p, hh=hh,
                                nb=nb: e.activation(
                                    scr(dst + 2048 * hh + 512 * nb,
                                        dst + 2048 * hh + 512 * (nb + 1),
                                        0, 64),
                                    P0[0:64, 512 * nb:512 * (nb + 1)],
                                    AF.Identity,
                                    bias=scr(bof + p, bof + p + 1,
                                             64 * hh, 64 * (hh + 1)),
                                    scale=1.0),
                                inc="ac")
                b.wait("tensor", "ac", b.cnt["ac"])
                b.wait("tensor", "gm", gp_ones)
                vev_hist = [0] * NT
                for c in range(NT):
                    if c >= 2:
                        b.wait("tensor", "ve", vev_hist[c - 2])
                    for m in range(8):
                        b.op("tensor",
                             lambda e, c=c, m=m: e.matmul(
                                 P1[:, 512 * (c % 2):512 * (c % 2) + 128],
                                 scr(HBASE + 2048 * m + 128 * c,
                                     HBASE + 2048 * m + 128 * (c + 1)),
                                 scr(WSL + 2048 + 128 * m,
                                     WSL + 2048 + 128 * (m + 1)),
                                 start=(m == 0), stop=(m == 7)),
                             inc="pe")
                    pev = b.cnt["pe"]
                    b.wait("vector", "pe", pev)
                    for hh in range(2):
                        vev_hist[c] = b.op("vector",
                                   lambda e, c=c, hh=hh, p=p: e.tensor_tensor(
                                       SCR[:, VS + 130 * c + 65 * hh:
                                           VS + 130 * c + 65 * hh + 64],
                                       P1[:, 512 * (c % 2) + 64 * hh:
                                          512 * (c % 2) + 64 * hh + 64],
                                       SCR[:, BVR + 128 * p + 64 * hh:
                                           BVR + 128 * p + 64 * hh + 64],
                                       ALU.add),
                                   inc="ve")
                v_done = vev_hist[NT - 1]
                if p == NPAIR - 1:
                    b.wait("sync", "pe", b.cnt["pe"])
                    for m_ in range(8):
                        wot_ld = b.dma(
                            "sync",
                            SCR[:, HBASE + 1024 * m_:HBASE + 1024 * (m_ + 1)],
                            wot[128 * m_:128 * (m_ + 1), :], "sx")

                pe_qr = [0] * NIT
                ac_ev = [0] * NIT
                sy_wr = [0] * NIT
                gp_rd = [0] * NIT
                ac_exp = [0] * NIT
                ve_nrm = [0] * NIT
                ve_aot = [0] * NIT
                pe_pv = [0] * NIT
                pv_ve = [0] * NIT
                ve_cp = [[0, 0] for _ in range(NIT)]
                pe_half = [[0, 0] for _ in range(NIT)]

                def it_hq(i):
                    return (i % 2, i // 2)

                def emit_qr(i):
                    h, q = it_hq(i)
                    w0 = GTT + 1920 - 128 * q
                    if i >= 1:
                        b.wait("tensor", "ac", ac_ev[i - 1])
                    offs = [1024, 1536, 2048, 0, 512]
                    wds = [512, 512, 127, 512, 512]
                    for ci, (o, wd) in enumerate(zip(offs, wds)):
                        if ci == 3 and i >= DEPTH:
                            b.wait("tensor", "ve", ve_cp[i - DEPTH][1])
                        b.op("tensor",
                             lambda e, h=h, q=q, o=o, wd=wd, w0=w0: e.matmul(
                                 P0[:, o:o + wd],
                                 SCR[0:64, QF + 2048 * h + 128 * q:
                                     QF + 2048 * h + 128 * (q + 1)],
                                 SCR[0:64, w0 + o:w0 + o + wd],
                                 start=True, stop=True),
                             inc="pe")
                    pe_qr[i] = b.cnt["pe"]

                def emit_evac(i):
                    b.wait("scalar", "pe", pe_qr[i])
                    if i >= 2:
                        b.wait("scalar", ("qwa", "qwb", "qwc")[(i - 2) % 3],
                               sy_wr[i - 2])
                    ac_ev[i] = b.op("scalar",
                                    lambda e, i=i: e.activation(
                                        SCR[:, sqr_off(i):sqr_off(i) + QW],
                                        P0[:, 0:QW], AF.Copy, scale=1.0),
                                    inc="ac")

                def emit_wr(i):
                    b.wait("sync", "ac", ac_ev[i])
                    if i >= 3:
                        b.wait("sync", ("gra", "grb", "grc")[(i - 3) % 3],
                               gp_rd[i - 3])
                    sy_wr[i] = b.dma("sync",
                                     qsl[128 * (i % 3):128 * (i % 3) + 128, :],
                                     SCR[:, sqr_off(i):sqr_off(i) + QW],
                                     ("qwa", "qwb", "qwc")[i % 3])

                def emit_rd(i):
                    b.wait("gpsimd", ("qwa", "qwb", "qwc")[i % 3], sy_wr[i])
                    if i >= 3:
                        b.wait("gpsimd", "pe", pe_half[i - 3][1])
                    src = AP(qsl.tensor, (i % 3) * 128 * QW + 127,
                             [[QW - 1, 128], [1, 2048]])
                    gp_rd[i] = b.dma("gpsimd",
                                     SCR[:, SPOS + 2048 * (i % 3):
                                         SPOS + 2048 * (i % 3) + 2048],
                                     src, ("gra", "grb", "grc")[i % 3])

                def emit_half(i, hf):
                    h, q = it_hq(i)
                    PS = P1 if hf == 0 else P0
                    b.wait("tensor", ("gra", "grb", "grc")[i % 3], gp_rd[i])
                    if hf == 0:
                        if i >= 1:
                            b.wait("tensor", "ve", pv_ve[i - 1])
                        else:
                            b.wait("tensor", "ve", v_done)
                    else:
                        b.wait("tensor", "ac", ac_ev[min(i + DEPTH - 1,
                                                         NIT - 1)])
                    jseq = (0, 4, 1, 5, 2, 6, 3, 7)
                    for j in jseq:
                        kc = 8 * hf + j
                        b.op("tensor",
                             lambda e, i=i, kc=kc, j=j, PS=PS: e.matmul(
                                 PS[:, 128 * j:128 * (j + 1)],
                                 SCR[:, SPOS + 2048 * (i % 3) + 128 * kc:
                                     SPOS + 2048 * (i % 3) + 128 * (kc + 1)],
                                 IDENT[:, :], start=(j == 0 or j == 4),
                                 stop=False,
                                 is_transpose=True),
                             inc="pe")
                    for j in jseq:
                        kc = 8 * hf + j
                        b.op("tensor",
                             lambda e, h=h, q=q, kc=kc, j=j, PS=PS: e.matmul(
                                 PS[:, 128 * j:128 * (j + 1)],
                                 SCR[0:64, KF + 2048 * h + 128 * kc:
                                     KF + 2048 * h + 128 * (kc + 1)],
                                 SCR[0:64, QF + 2048 * h + 128 * q:
                                     QF + 2048 * h + 128 * (q + 1)],
                                 start=False, stop=(j == 3 or j == 7)),
                             inc="pe")
                    pe_half[i][hf] = b.cnt["pe"]
                    b.wait("vector", "pe", pe_half[i][hf])
                    ve_cp[i][hf] = b.op("vector",
                                        lambda e, hf=hf, PS=PS: e.tensor_copy(
                                            SCR[:, ET + 1024 * hf:
                                                ET + 1024 * (hf + 1)],
                                            PS[:, 0:1024]),
                                        inc="ve")

                def emit_exp(i, hf):
                    b.wait("scalar", "ve", ve_cp[i][hf])
                    ac_exp[i] = b.op("scalar",
                                     lambda e, hf=hf: e.activation(
                                         SCR[:, ET + 1024 * hf:
                                             ET + 1024 * (hf + 1)],
                                         SCR[:, ET + 1024 * hf:
                                             ET + 1024 * (hf + 1)],
                                         AF.Exp, scale=0.125),
                                     inc="ac")

                def emit_pv(i):
                    h, q = it_hq(i)
                    b.wait("tensor", "ac", ac_exp[i])
                    if i >= 1:
                        b.wait("tensor", "ve", ve_aot[i - 1])
                    voff = 0 if h == 0 else 2
                    for kc in range(16):
                        b.op("tensor",
                             lambda e, voff=voff, kc=kc: e.matmul(
                                 P2[:, 0:128] if kc % 2 == 0 else P1[:, 0:128],
                                 SCR[:, ET + 128 * kc:ET + 128 * (kc + 1)],
                                 SCR[:, VS + 130 * kc + voff:
                                     VS + 130 * kc + voff + 128],
                                 start=(kc < 2), stop=(kc >= 14)),
                             inc="pe")
                    pe_pv[i] = b.cnt["pe"]

                def emit_norm(i):
                    h, q = it_hq(i)
                    pv0 = 0 if h == 0 else 63
                    zc = 64 if h == 0 else 127
                    b.wait("vector", "pe", pe_pv[i])
                    vc = b.op("vector",
                              lambda e: e.tensor_copy(scr(PVT, PVT + 128),
                                                      P1[:, 0:128]),
                              inc="ve")
                    pv_ve[i] = vc
                    b.wait("vector", "ve", vc)
                    v0 = b.op("vector",
                              lambda e: e.tensor_tensor(
                                  scr(PVT, PVT + 128),
                                  P2[:, 0:128], scr(PVT, PVT + 128), ALU.add),
                              inc="ve")
                    b.wait("vector", "ve", v0)
                    r1 = b.op("vector",
                              lambda e, zc=zc: e.reciprocal(
                                  scr(SMALL, SMALL + 1),
                                  scr(PVT + zc, PVT + zc + 1)),
                              inc="ve")
                    b.wait("vector", "ve", r1)
                    ve_nrm[i] = b.op("vector",
                                     lambda e, pv0=pv0: e.tensor_scalar(
                                         scr(AOV, AOV + 64),
                                         scr(PVT + pv0, PVT + pv0 + 64),
                                         scr(SMALL, SMALL + 1), None,
                                         ALU.mult),
                                     inc="ve")

                def emit_aot(i):
                    h, q = it_hq(i)
                    b.wait("tensor", "ve", ve_nrm[i])
                    b.wait("vector", "aow", b.cnt["aow"])
                    b.op("tensor",
                         lambda e: e.matmul(P2[0:64, 128:256],
                                            scr(AOV, AOV + 64),
                                            IDENT[:, :], start=True, stop=True,
                                            is_transpose=True),
                         inc="pe")
                    pea = b.cnt["pe"]
                    b.wait("vector", "pe", pea)
                    ve_aot[i] = b.op("vector",
                                     lambda e: e.tensor_copy(
                                         scr(AOT, AOT + 128, 0, 64),
                                         P2[0:64, 128:256]),
                                     inc="ve")
                    b.wait("gpsimd", "ve", ve_aot[i])
                    b.dma("gpsimd",
                          AP(aofm.tensor, (2 * p + h) * 64 * S + 128 * q,
                             [[S, 64], [1, 128]]),
                          scr(AOT, AOT + 128, 0, 64), "aow")

                for i in range(min(DEPTH, NIT)):
                    emit_qr(i); emit_evac(i); emit_wr(i); emit_rd(i)
                for i in range(NIT):
                    emit_half(i, 0)
                    emit_exp(i, 0)
                    if i >= 1:
                        emit_aot(i - 1)
                    emit_half(i, 1)
                    emit_exp(i, 1)
                    if i + DEPTH < NIT:
                        emit_qr(i + DEPTH); emit_evac(i + DEPTH)
                        emit_wr(i + DEPTH); emit_rd(i + DEPTH)
                    emit_pv(i)
                    emit_norm(i)
                emit_aot(NIT - 1)

            AOF = XA
            XTO = HBASE + 8192
            barrier("sync")
            barrier("gpsimd")
            lda = None
            for m in range(8):
                lda = b.dma("gpsimd",
                            SCR[:, AOF + 2048 * m:AOF + 2048 * (m + 1)],
                            aofm[128 * m:128 * (m + 1), :], "gra")
            b.wait("tensor", "sx", wot_ld)
            b.wait("tensor", "gra", lda)
            b.wait("tensor", "ac", b.cnt["ac"])
            st_last = None
            xt_pre = [b.dma("sync", SCR[:, XTO:XTO + 1024],
                            src_hbm[0:128, :], "xta")]
            v_hist = [0] * NT
            sto_hist = [0] * NT
            for t in range(NT):
                if t + 1 < NT:
                    if t >= 1:
                        b.wait("sync", "so", sto_hist[t - 1])
                    xt_pre.append(b.dma("sync",
                                        SCR[:, XTO + 1024 * ((t + 1) % 2):
                                            XTO + 1024 * ((t + 1) % 2) + 1024],
                                        src_hbm[128 * (t + 1):128 * (t + 2), :],
                                        "xta" if (t + 1) % 2 == 0 else "xtb"))
                if t >= 1:
                    b.wait("tensor", "ve", v_hist[t - 1])
                for nb in range(2):
                    for m in range(8):
                        b.op("tensor",
                             lambda e, t=t, nb=nb, m=m: e.matmul(
                                 P1[:, 512 * nb:512 * (nb + 1)],
                                 SCR[:, AOF + 2048 * m + 128 * t:
                                     AOF + 2048 * m + 128 * (t + 1)],
                                 SCR[:, HBASE + 1024 * m + 512 * nb:
                                     HBASE + 1024 * m + 512 * (nb + 1)],
                                 start=(m == 0), stop=(m == 7)),
                             inc="pe")
                peo = b.cnt["pe"]
                b.wait("vector", "pe", peo)
                b.wait("vector", "xta" if t % 2 == 0 else "xtb", xt_pre[t])
                vtmp = b.op("vector",
                     lambda e, t=t: e.tensor_tensor(
                         SCR[:, XTO + 1024 * (t % 2):XTO + 1024 * (t % 2) + 1024],
                         P1[:, 0:1024],
                         SCR[:, XTO + 1024 * (t % 2):XTO + 1024 * (t % 2) + 1024],
                         ALU.add),
                     inc="ve")
                b.wait("vector", "ve", vtmp)
                v_hist[t] = b.op("vector",
                                 lambda e, t=t: e.tensor_tensor(
                                     SCR[:, XTO + 1024 * (t % 2):
                                         XTO + 1024 * (t % 2) + 1024],
                                     SCR[:, XTO + 1024 * (t % 2):
                                         XTO + 1024 * (t % 2) + 1024],
                                     SCR[:, BOR:BOR + 1024], ALU.add),
                                 inc="ve")
                b.wait("sync", "ve", v_hist[t])
                st_last = b.dma("sync", dst_hbm[128 * t:128 * (t + 1), :],
                                SCR[:, XTO + 1024 * (t % 2):
                                    XTO + 1024 * (t % 2) + 1024], "so")
                sto_hist[t] = st_last
            return st_last

        for eng in ("vector", "scalar", "tensor", "gpsimd"):
            b.wait(eng, "ld", consts_ld)

        cur = x
        if DBG in ("ln", "lnfm"):
            hv = emit_ln(x, "tok" if DBG == "ln" else "fm")
            barrier("sync")
            for t in range(NT):
                b.wait("sync", "so", b.cnt["so"])
                b.dma("sync", out[128 * t:128 * (t + 1), :],
                      SCR[:, HBASE + 1024 * t:HBASE + 1024 * (t + 1)], "so")
            b.wait("sync", "so", b.cnt["so"])
            emit_block_end = True
        else:
            emit_block_end = False
        if not emit_block_end and DO_FFN1:
            hv = emit_ln(x, "tok")
            emit_ffn(x, xs1, hv)
            cur = xs1
        if not emit_block_end and DO_ATTN:
            hv = emit_ln(cur, "fm")
            emit_attn(cur, xs2 if DO_FFN2 else out, hv)
            cur = xs2 if DO_FFN2 else None
        if not emit_block_end and DO_FFN2:
            hv = emit_ln(cur if cur is not None else x, "tok")
            emit_ffn(cur, out, hv)

        if (not emit_block_end and DBG == "" and DO_FFN1
                and not DO_ATTN and not DO_FFN2):
            barrier("sync")
            for t in range(NT):
                b.wait("sync", "so", b.cnt["so"])
                lv = b.dma("sync", SCR[:, XA:XA + 1024],
                           xs1[128 * t:128 * (t + 1), :], "sx")
                b.wait("sync", "sx", lv)
                b.dma("sync", out[128 * t:128 * (t + 1), :],
                      SCR[:, XA:XA + 1024], "so")

        b.wait("sync", "so", b.cnt["so"])

        with nc.Block() as block:
            @block.sync
            def _(sync):
                for f in b.th["sync"]:
                    f(sync)

            @block.gpsimd
            def _(gpsimd):
                for f in b.th["gpsimd"]:
                    f(gpsimd)

            @block.tensor
            def _(tensor):
                for f in b.th["tensor"]:
                    f(tensor)

            @block.vector
            def _(vector):
                for f in b.th["vector"]:
                    f(vector)

            @block.scalar
            def _(scalar):
                for f in b.th["scalar"]:
                    f(scalar)

    return nc


def _host_prep(inputs):
    g = {}
    g["wqt"] = np.asarray(inputs["L0_attn_Wq"]).T
    g["wkt"] = np.asarray(inputs["L0_attn_Wk"]).T
    g["wvt"] = np.asarray(inputs["L0_attn_Wv"]).T
    g["wot"] = np.asarray(inputs["L0_attn_Wo"]).T
    g["bqs"] = np.asarray(inputs["L0_attn_bq"]).reshape(8, 128).T
    g["bks"] = np.asarray(inputs["L0_attn_bk"]).reshape(8, 128).T
    g["bvrep"] = np.broadcast_to(np.asarray(inputs["L0_attn_bv"]), (128, DM))
    g["borep"] = np.broadcast_to(np.asarray(inputs["L0_attn_bo"]), (128, DM))
    tbl = np.asarray(inputs["L0_attn_rel_table"])
    idx = np.clip(3070 - np.arange(4095), 0, 2046)
    gtt = tbl[idx].T
    g["gttd"] = np.concatenate([gtt, gtt], axis=0)
    ctrl = np.asarray(inputs["L0_ffn_controller"]).reshape(DM, NES)
    g["ctrlc"] = ctrl.reshape(8, 128, NES).transpose(1, 0, 2).reshape(128, 1024)
    g["f1r"] = np.asarray(inputs["L0_ffn_f1"]).reshape(DM, NES * FF)
    g["f1b"] = np.asarray(inputs["L0_ffn_bias"]).reshape(1, NES * FF)
    g["f1brep"] = np.broadcast_to(g["f1b"], (32, NES * FF))
    g["f2r"] = np.asarray(inputs["L0_ffn_f2"]).reshape(NES * FF, DM)
    g["identh"] = np.eye(128, dtype=np.float32)
    tie = np.linspace(0.0, 1e-6, TG, dtype=np.float32)
    g["tie4h"] = np.broadcast_to(np.tile(tie, 4), (128, 128))
    for k in g:
        g[k] = np.ascontiguousarray(g[k], dtype=np.float32)
    return g


def kernel(**inputs):
    xfull = np.ascontiguousarray(np.asarray(inputs["L0_attn_x"], dtype=np.float32))
    shards = [np.ascontiguousarray(xfull[i]) for i in range(8)]
    shared = _host_prep(inputs)
    nc = build_nc()
    in_maps = [dict(shared, x=shards[i]) for i in range(8)]
    res = run_bass_kernel_spmd(nc, in_maps, core_ids=list(range(8)))
    outs = [np.asarray(r["out"]) for r in res.results]
    return np.stack(outs, axis=0).astype(np.float32)


# version 18 (final)
# speedup vs baseline: 1.0428x; 1.0428x over previous
import numpy as np
from contextlib import ExitStack

import concourse.bass as bass
import concourse.mybir as mybir
from concourse.bass_utils import run_bass_kernel_spmd
from concourse.ap import AP

F32 = mybir.dt.float32
AF = mybir.ActivationFunctionType
ALU = mybir.AluOpType
AX = mybir.AxisListType

S = 2048
DM = 1024
NPAIR = 8
NT = 16
NES = 128
TG = 32
FF = 32
NG = 64
QW = 2175
SCRW = 49152

DO_FFN1 = True
DO_ATTN = True
DO_FFN2 = True
DBG = ""


class Bld:
    ENGS = ("sync", "gpsimd", "tensor", "vector", "scalar")

    def __init__(self, nc, stack):
        self.nc = nc
        self.stack = stack
        self.th = {e: [] for e in self.ENGS}
        self.sems = {}
        self.cnt = {}

    def sem(self, name):
        self.sems[name] = self.stack.enter_context(self.nc.semaphore(name))
        self.cnt[name] = 0

    def wait(self, eng, semname, val):
        if val is None or val <= 0:
            return
        s = self.sems[semname]
        self.th[eng].append(lambda e, s=s, v=val: e.wait_ge(s, v))

    def op(self, eng, fn, inc=None, n=1):
        if inc is None:
            self.th[eng].append(fn)
            return None
        self.cnt[inc] += n
        s = self.sems[inc]
        self.th[eng].append(lambda e, fn=fn, s=s, n=n: fn(e).then_inc(s, n))
        return self.cnt[inc]

    def dma(self, eng, out, in_, inc):
        return self.op(eng, lambda e, o=out, i=in_: e.dma_start(out=o, in_=i),
                       inc=inc, n=16)


def build_nc():
    nc = bass.Bass("TRN2", target_bir_lowering=False)

    x = nc.dram_tensor("x", [S, DM], F32, kind="ExternalInput").ap()
    wqt = nc.dram_tensor("wqt", [DM, DM], F32, kind="ExternalInput").ap()
    wkt = nc.dram_tensor("wkt", [DM, DM], F32, kind="ExternalInput").ap()
    wvt = nc.dram_tensor("wvt", [DM, DM], F32, kind="ExternalInput").ap()
    wot = nc.dram_tensor("wot", [DM, DM], F32, kind="ExternalInput").ap()
    bqs = nc.dram_tensor("bqs", [128, 8], F32, kind="ExternalInput").ap()
    bks = nc.dram_tensor("bks", [128, 8], F32, kind="ExternalInput").ap()
    bvrep = nc.dram_tensor("bvrep", [128, DM], F32, kind="ExternalInput").ap()
    borep = nc.dram_tensor("borep", [128, DM], F32, kind="ExternalInput").ap()
    gttd = nc.dram_tensor("gttd", [128, 4095], F32, kind="ExternalInput").ap()
    ctrlc = nc.dram_tensor("ctrlc", [128, 1024], F32, kind="ExternalInput").ap()
    f1r = nc.dram_tensor("f1r", [DM, NES * FF], F32, kind="ExternalInput").ap()
    f1b = nc.dram_tensor("f1b", [1, NES * FF], F32, kind="ExternalInput").ap()
    f1brep = nc.dram_tensor("f1brep", [32, NES * FF], F32, kind="ExternalInput").ap()
    f2r = nc.dram_tensor("f2r", [NES * FF, DM], F32, kind="ExternalInput").ap()
    identh = nc.dram_tensor("identh", [128, 128], F32, kind="ExternalInput").ap()
    tie4h = nc.dram_tensor("tie4h", [128, 128], F32, kind="ExternalInput").ap()

    out = nc.dram_tensor("out", [S, DM], F32, kind="ExternalOutput").ap()

    xs1 = nc.dram_tensor("xs1", [S, DM], F32).ap()
    xs2 = nc.dram_tensor("xs2", [S, DM], F32).ap()
    aofm = nc.dram_tensor("aofm", [DM, S], F32).ap()
    qsl = nc.dram_tensor("qsl", [3 * 128, QW], F32).ap()
    itb = nc.dram_tensor("itb", [NES * NG, DM], F32).ap()
    hsc = nc.dram_tensor("hsc", [S, DM], F32).ap()

    with ExitStack() as stack:
        SCR = stack.enter_context(nc.sbuf_tensor("SCR", [128, SCRW], F32))
        IDENT = stack.enter_context(nc.sbuf_tensor("IDENT", [128, 128], F32))
        TIE4 = stack.enter_context(nc.sbuf_tensor("TIE4", [128, 128], F32))
        EPS = stack.enter_context(nc.sbuf_tensor("EPS", [128, 1], F32))
        P0 = stack.enter_context(nc.psum_tensor("P0", [128, 2560], F32))
        P1 = stack.enter_context(nc.psum_tensor("P1", [128, 1024], F32))
        P2 = stack.enter_context(nc.psum_tensor("P2", [128, 512], F32))

        b = Bld(nc, stack)
        for s_ in ("ld", "sx", "so", "fla", "flb", "fba", "fbb", "fbc", "fbd",
                   "gm", "qwa", "qwb", "qwc", "gra", "grb", "grc",
                   "iwa", "iwb", "rha", "rhb", "xta", "xtb", "aow",
                   "hw", "hr", "lxa", "lxb", "lxc", "lxd",
                   "rga", "rgb", "rgc", "rgd", "rhc", "rhd",
                   "xsa", "xsb", "pe", "ac", "ve"):
            b.sem(s_)
        DMA_SEMS = ("ld", "sx", "so", "fla", "flb", "fba", "fbb", "fbc", "fbd",
                    "qwa", "qwb", "qwc",
                    "gra", "grb", "grc", "iwa", "iwb", "rha", "rhb",
                    "xta", "xtb", "aow", "hw", "hr", "lxa", "lxb", "lxc", "lxd",
                    "rga", "rgb", "rgc", "rgd", "rhc", "rhd",
                    "xsa", "xsb")

        def barrier(eng):
            for s_ in DMA_SEMS + ("gm", "pe", "ac", "ve"):
                b.wait(eng, s_, b.cnt[s_])

        def scr(lo, hi, p0=0, p1=128):
            return SCR[p0:p1, lo:hi]

        GTT = 0
        CT = 4096
        BVR = 5120
        BOR = 6144
        BQ = 7168
        BK = 7176
        STT = 7184
        ONES = 7264
        SMALL = 7280
        XA = 7296
        HBASE = XA + 16384
        TAIL = HBASE + 16384
        assert TAIL + 8448 <= SCRW

        b.dma("sync", SCR[:, GTT:GTT + 4095], gttd[:, :], "ld")
        b.dma("sync", SCR[:, CT:CT + 1024], ctrlc[:, :], "ld")
        b.dma("sync", SCR[:, BVR:BVR + 1024], bvrep[:, :], "ld")
        b.dma("sync", SCR[:, BOR:BOR + 1024], borep[:, :], "ld")
        b.dma("sync", SCR[:, BQ:BQ + 8], bqs[:, :], "ld")
        b.dma("sync", SCR[:, BK:BK + 8], bks[:, :], "ld")
        b.dma("sync", IDENT[:, :], identh[:, :], "ld")
        b.dma("sync", TIE4[:, :], tie4h[:, :], "ld")
        consts_ld = b.cnt["ld"]
        b.op("gpsimd", lambda e: e.memset(EPS[:, :], 1e-5), inc="gm", n=1)
        b.op("gpsimd", lambda e: e.memset(scr(ONES, ONES + 64, 0, 1), 1.0),
             inc="gm", n=1)
        consts_gp = b.cnt["gm"]

        def emit_ln(src_hbm, mode):
            barrier("sync")
            LXS = ("lxa", "lxb", "lxc", "lxd")
            lx_ld = []
            for ch in range(4):
                src_ap = AP(src_hbm.tensor, src_hbm.offset + 512 * DM * ch,
                            [[DM, 128], [128 * DM, 4], [1, DM]])
                lx_ld.append(b.dma("sync",
                                   AP(SCR[:, :].tensor, XA + 4096 * ch,
                                      [[SCRW, 128], [DM, 4], [1, DM]]),
                                   src_ap, LXS[ch]))
            vsum = 0
            for ch in range(4):
                b.wait("vector", LXS[ch], lx_ld[ch])
                vsum = b.op("vector",
                            lambda e, ch=ch: e.reduce_sum(
                                scr(STT + 4 * ch, STT + 4 * ch + 4),
                                AP(SCR[:, :].tensor, XA + 4096 * ch,
                                   [[SCRW, 128], [DM, 4], [1, DM]]),
                                axis=AX.X),
                            inc="ve")
            for t in range(NT):
                b.wait("scalar", LXS[t // 4], lx_ld[t // 4])
                b.op("scalar",
                     lambda e, t=t: e.activation(
                         SCR[:, HBASE + 1024 * t:HBASE + 1024 * (t + 1)],
                         scr(XA + 1024 * t, XA + 1024 * (t + 1)),
                         AF.Square, accum_out=scr(STT + 16 + t, STT + 17 + t)),
                     inc="ac")
            acs = b.cnt["ac"]
            b.wait("vector", "ve", vsum)
            vmean = b.op("vector",
                         lambda e: e.tensor_scalar(
                             scr(STT + 32, STT + 48), scr(STT, STT + 16),
                             1.0 / DM, None, ALU.mult),
                         inc="ve")
            b.wait("vector", "ve", vmean)
            vmsq = b.op("vector",
                        lambda e: e.tensor_tensor(
                            scr(SMALL, SMALL + 16), scr(STT + 32, STT + 48),
                            scr(STT + 32, STT + 48), ALU.mult),
                        inc="ve")
            b.wait("vector", "ac", acs)
            b.wait("vector", "ve", vmsq)
            vvar = b.op("vector",
                        lambda e: e.scalar_tensor_tensor(
                            scr(STT + 48, STT + 64), scr(STT + 16, STT + 32),
                            1.0 / DM, scr(SMALL, SMALL + 16),
                            ALU.mult, ALU.subtract),
                        inc="ve")
            b.wait("scalar", "ve", vvar)
            b.wait("scalar", "gm", consts_gp)
            a1 = b.op("scalar",
                      lambda e: e.activation(scr(SMALL, SMALL + 16),
                                             scr(STT + 48, STT + 64),
                                             AF.Ln, bias=EPS[:, 0:1], scale=1.0),
                      inc="ac")
            b.wait("scalar", "ac", a1)
            a2 = b.op("scalar",
                      lambda e: e.activation(scr(STT + 64, STT + 80),
                                             scr(SMALL, SMALL + 16),
                                             AF.Exp, scale=-0.5),
                      inc="ac")
            b.wait("vector", "ac", a2)
            if mode == "tok":
                last = None
                for t in range(NT):
                    last = b.op("vector",
                                lambda e, t=t: e.tensor_scalar(
                                    scr(HBASE + 1024 * t, HBASE + 1024 * (t + 1)),
                                    scr(XA + 1024 * t, XA + 1024 * (t + 1)),
                                    scr(STT + 32 + t, STT + 33 + t),
                                    scr(STT + 64 + t, STT + 65 + t),
                                    ALU.subtract, ALU.mult),
                                inc="ve")
                    if t >= 8:
                        b.wait("sync", "ve", last)
                        b.dma("sync", hsc[128 * t:128 * (t + 1), :],
                              SCR[:, HBASE + 1024 * t:HBASE + 1024 * (t + 1)],
                              "hw")
                return last
            for t in range(NT):
                ht = b.op("vector",
                          lambda e, t=t: e.tensor_scalar(
                              scr(XA + 1024 * t, XA + 1024 * (t + 1)),
                              scr(XA + 1024 * t, XA + 1024 * (t + 1)),
                              scr(STT + 32 + t, STT + 33 + t),
                              scr(STT + 64 + t, STT + 65 + t),
                              ALU.subtract, ALU.mult),
                          inc="ve")
                b.wait("tensor", "ve", ht)
                b.wait("tensor", "ve", b.cnt["ve"])
                for m in range(8):
                    b.op("tensor",
                         lambda e, t=t, m=m: e.matmul(
                             P1[:, 128 * m:128 * (m + 1)],
                             scr(XA + 1024 * t + 128 * m,
                                 XA + 1024 * t + 128 * (m + 1)),
                             IDENT[:, :], start=True, stop=True,
                             is_transpose=True),
                         inc="pe")
                pev = b.cnt["pe"]
                b.wait("vector", "pe", pev)
                b.op("vector",
                     lambda e, t=t: e.tensor_copy(
                         AP(SCR[:, :].tensor, HBASE + 128 * t,
                            [[SCRW, 128], [2048, 8], [1, 128]]),
                         P1[:, 0:1024]),
                     inc="ve")
            return b.cnt["ve"]

        def emit_ffn(src_hbm, dst_hbm, h_ve):
            PRMP = XA + 0
            HFM = XA + 8192
            STG = XA + 8192
            INT = XA + 9216
            ITS = XA + 9728
            F1BB = XA + 10240
            F1BUF = XA + 10368
            F2BUF = XA + 11392
            F2B = XA + 14464
            ITS2 = XA + 12416
            XT = XA + 10240
            DS = HBASE + 8192
            RHS2 = TAIL

            b.wait("vector", "ve", h_ve)
            b.wait("tensor", "ve", h_ve)
            b.wait("scalar", "ve", h_ve)

            vz = b.op("vector",
                      lambda e: e.memset(scr(PRMP, PRMP + 8192), 0.0),
                      inc="ve")

            for t in range(NT):
                b.wait("tensor", "ve", b.cnt["ve"])
                for m in range(8):
                    b.op("tensor",
                         lambda e, t=t, m=m: e.matmul(
                             P1[:, 128 * m:128 * (m + 1)],
                             scr(HBASE + 1024 * t + 128 * m,
                                 HBASE + 1024 * t + 128 * (m + 1)),
                             IDENT[:, :], start=True, stop=True,
                             is_transpose=True),
                         inc="pe")
                pet = b.cnt["pe"]
                b.wait("vector", "pe", pet)
                vh = b.op("vector",
                          lambda e: e.tensor_copy(scr(HFM, HFM + 1024),
                                                  P1[:, 0:1024]),
                          inc="ve")
                b.wait("tensor", "ve", vh)
                for m in range(8):
                    b.op("tensor",
                         lambda e, m=m: e.matmul(
                             P2[:, 0:128],
                             scr(CT + 128 * m, CT + 128 * (m + 1)),
                             scr(HFM + 128 * m, HFM + 128 * (m + 1)),
                             start=(m == 0), stop=(m == 7)),
                         inc="pe")
                pel = b.cnt["pe"]
                b.wait("vector", "pe", pel)
                v1 = b.op("vector",
                          lambda e: e.tensor_tensor(scr(SMALL + 4, SMALL + 4 + 128)
                                                    if False else
                                                    scr(STG, STG + 128),
                                                    P2[:, 0:128], TIE4[:, :],
                                                    ALU.add),
                          inc="ve")
                b.wait("vector", "ve", v1)
                v2 = b.op("vector",
                          lambda e: e.reduce_max(
                              scr(SMALL, SMALL + 4),
                              AP(SCR[:, :].tensor, STG,
                                 [[SCRW, 128], [32, 4], [1, 32]]),
                              axis=AX.X),
                          inc="ve")
                b.wait("vector", "ve", v2)
                if t == 0:
                    b.wait("vector", "ve", vz)
                for a in range(4):
                    b.op("vector",
                         lambda e, a=a, t=t: e.tensor_scalar(
                             scr(PRMP + 512 * t + 160 * a,
                                 PRMP + 512 * t + 160 * a + 32),
                             scr(STG + 32 * a, STG + 32 * (a + 1)),
                             scr(SMALL + a, SMALL + a + 1), None, ALU.is_equal),
                         inc="ve")
            vperm_done = b.cnt["ve"]
            if DBG == "ffa":
                barrier("sync")
                for t in range(8):
                    b.dma("sync", out[128 * t:128 * (t + 1), :],
                          SCR[:, PRMP + 1024 * t:PRMP + 1024 * (t + 1)], "so")
                for t in range(2):
                    b.dma("sync", out[128 * (8 + t):128 * (9 + t), :],
                          SCR[:, PRT + 1024 * t:PRT + 1024 * (t + 1)], "so")
                b.dma("sync", out[128 * 10:128 * 11, :],
                      SCR[:, STG:STG + 1024], "so")
                b.dma("sync", out[128 * 11:128 * 12, :],
                      SCR[:, HBASE + 1024 * 15:HBASE + 1024 * 16], "so")
                b.dma("sync", out[128 * 12:128 * 13, :],
                      SCR[:, HFM:HFM + 1024], "so")
                b.dma("sync", out[128 * 14:128 * 15, :],
                      SCR[:, CT:CT + 1024], "so")
                b.dma("sync", out[128 * 15:128 * 16, :],
                      SCR[:, XA + 1024 * 15:XA + 1024 * 16], "so")
                b.wait("sync", "so", b.cnt["so"])
                return None

            b.wait("tensor", "ve", vperm_done)
            b.wait("vector", "ve", vperm_done)
            zpad = 0
            for (lo, hi) in ((STG, STG + 1024), (INT, INT + 512),
                             (F2BUF, F2BUF + 1024), (F2B, F2B + 1024),
                             (GTT + 1152, GTT + 2176), (GTT + 2176, GTT + 3200)):
                for (p0, p1) in ((32, 64), (64, 128)):
                    zpad = b.op("vector",
                                lambda e, lo=lo, hi=hi, p0=p0, p1=p1: e.memset(
                                    scr(lo, hi, p0, p1), 0.0),
                                inc="ve", n=1)
            b.wait("tensor", "ve", zpad)

            vf1_prev = 0
            vT_prev = 0
            pef1_prev = 0
            vb_prev = 0
            gp_prev_it = [0, 0]
            dve = b.cnt["ve"]
            F2S = (F2BUF, F2B, GTT + 1152, GTT + 2176)
            F2SEM = ("fba", "fbb", "fbc", "fbd")
            es_seq = [64 * grp_ + 4 * c_ + e4_
                      for gh_ in range(2) for grp_ in range(2)
                      for c_ in range(16) for e4_ in range(4)]
            f2_rd = [0, 0, 0, 0]
            f2_ld = [0, 0, 0, 0]
            ac_hist = [0] * (len(es_seq) + 2)
            f2_idx = 0

            def load_f2(j):
                if j >= len(es_seq):
                    return
                slot = j % 4
                b.wait("sync", "pe", f2_rd[slot])
                f2_ld[slot] = b.dma(
                    "sync", scr(F2S[slot], F2S[slot] + 1024, 0, 32),
                    f2r[32 * es_seq[j]:32 * es_seq[j] + 32, :], F2SEM[slot])

            F1SL = (F1BUF, GTT)
            F1BBSL = (F1BB, GTT + 1024)
            F1SEM = ("fla", "flb")
            es0_seq = [64 * grp_ + 4 * c_
                       for gh_ in range(2) for grp_ in range(2)
                       for c_ in range(16)]
            NPK = len(es0_seq)
            pef1_hist = [0] * NPK
            vb_hist = [0] * NPK
            f1_ld = [0, 0]
            pk = 0

            def load_f1(j):
                if j >= NPK:
                    return
                slot = j % 2
                if j >= 2:
                    b.wait("sync", "pe", pef1_hist[j - 2])
                    b.wait("sync", "ve", vb_hist[j - 2])
                es0_ = es0_seq[j]
                b.dma("sync",
                      AP(SCR[:, :].tensor, F1SL[slot],
                         [[SCRW, 128], [128, 8], [1, 128]]),
                      AP(f1r.tensor, 32 * es0_,
                         [[4096, 128], [4096 * 128, 8], [1, 128]]),
                      F1SEM[slot])
                f1_ld[slot] = b.dma(
                    "sync", scr(F1BBSL[slot], F1BBSL[slot] + 128, 0, 32),
                    f1brep[0:32, 32 * es0_:32 * es0_ + 128], F1SEM[slot])

            b.wait("sync", "ve", h_ve)
            load_f2(0)
            load_f2(1)
            load_f2(2)
            load_f2(3)
            load_f1(0)
            load_f1(1)
            hr_ld = [0] * 8
            b.wait("vector", "hw", b.cnt["hw"])
            for gh in range(2):
                for grp in range(2):
                    for ti in range(8):
                        t = 8 * gh + ti
                        hb = HBASE + 1024 * ti
                        if gh == 1 and ti == 0 and grp == 0:
                            b.wait("tensor", "hr", hr_ld[7])
                        b.wait("tensor", "ve", max(dve, vf1_prev, vT_prev))
                        if ti == 0:
                            b.wait("tensor", "ac", b.cnt["ac"])
                        for a in range(4):
                            b.op("tensor",
                                 lambda e, t=t, a=a: e.matmul(
                                     P2[:, 128 * a:128 * (a + 1)],
                                     scr(PRMP + 512 * t + 128 * a,
                                         PRMP + 512 * t + 128 * (a + 1)),
                                     IDENT[:, :], start=(a == 0), stop=(a == 3),
                                     is_transpose=True),
                                 inc="pe")
                        pez = b.cnt["pe"]
                        b.wait("vector", "pe", pez)
                        vz2 = b.op("vector",
                                   lambda e: e.tensor_copy(scr(ITS, ITS + 512),
                                                           P2[:, 0:512]),
                                   inc="ve")
                        b.wait("tensor", "ve", vz2)
                        for m in range(8):
                            for a in range(4):
                                b.op("tensor",
                                     lambda e, hb=hb, m=m, a=a, grp=grp: e.matmul(
                                         P0[:, 256 * m + 64 * a:
                                            256 * m + 64 * (a + 1)],
                                         scr(hb + 128 * m, hb + 128 * (m + 1)),
                                         scr(ITS + 128 * a + 64 * grp,
                                             ITS + 128 * a + 64 * grp + 64),
                                         start=(a == 0 and (m % 2) == 0),
                                         stop=(a == 3 and (m % 2) == 1)),
                                     inc="pe")
                            if m % 2 == 1:
                                bk2 = m // 2
                                ped = b.cnt["pe"]
                                b.wait("vector", "pe", ped)
                                dve = b.op("vector",
                                           lambda e, bk2=bk2, ti=ti: e.tensor_copy(
                                               AP(SCR[:, :].tensor,
                                                  DS + 4096 * bk2 + 4 * ti,
                                                  [[SCRW, 128], [2048, 2],
                                                   [1, 4], [32, 64]]),
                                               AP(P0[:, :].tensor,
                                                  P0[:, :].offset + 512 * bk2,
                                                  [[2560, 128], [256, 2],
                                                   [64, 4], [1, 64]])),
                                           inc="ve")
                    if gh == 0 and grp == 1:
                        b.wait("sync", "pe", b.cnt["pe"])
                        b.wait("sync", "hw", b.cnt["hw"])
                        for k in range(8):
                            hr_ld[k] = b.dma(
                                "sync",
                                SCR[:, HBASE + 1024 * k:HBASE + 1024 * (k + 1)],
                                hsc[128 * (8 + k):128 * (9 + k), :], "hr")
                    b.wait("tensor", "ve", dve)

                    def emit_f1(c):
                        nonlocal pk, vb_prev, vf1_prev
                        es0 = 64 * grp + 4 * c
                        cb = c % 8
                        f1slot = pk % 2
                        f1buf = F1SL[f1slot]
                        f1bb = F1BBSL[f1slot]
                        b.wait("tensor", F1SEM[f1slot], f1_ld[f1slot])
                        b.wait("tensor", "ve", vb_prev)
                        f1cols = []
                        for e4 in range(4):
                            es_loc = 4 * c + e4
                            col = 512 * (e4 % 2) + 32 * (e4 // 2)
                            f1cols.append(col)
                            for m in range(8):
                                b.op("tensor",
                                     lambda e, m=m, es_loc=es_loc, col=col,
                                     e4=e4, f1buf=f1buf: e.matmul(
                                         P1[0:32, col:col + 32],
                                         AP(SCR[:, :].tensor,
                                            DS + 2048 * m + 32 * es_loc,
                                            [[SCRW, 128], [1, 32]]),
                                         scr(f1buf + 128 * m + 32 * e4,
                                             f1buf + 128 * m + 32 * (e4 + 1)),
                                         start=(m == 0 and e4 < 2),
                                         stop=(m == 7 and e4 >= 2)),
                                     inc="pe")
                        pef1 = b.cnt["pe"]
                        pef1_hist[pk] = pef1
                        return dict(cb=cb, es0=es0, pef1=pef1,
                                    f1cols=f1cols, f1bb=f1bb,
                                    f1slot=f1slot, pk=pk)

                    def emit_bias(st):
                        nonlocal pk, vb_prev, vf1_prev
                        b.wait("vector", "pe", st["pef1"])
                        b.wait("vector", F1SEM[st["f1slot"]],
                               f1_ld[st["f1slot"]])
                        cb = st["cb"]
                        for e4 in range(4):
                            b.op("vector",
                                 lambda e, e4=e4, c=cb, col=st["f1cols"][e4],
                                 f1bb=st["f1bb"]: e.tensor_tensor(
                                     SCR[0:32, STG + 128 * c + 32 * e4:
                                         STG + 128 * c + 32 * (e4 + 1)],
                                     P1[0:32, col:col + 32],
                                     scr(f1bb + 32 * e4, f1bb + 32 * (e4 + 1),
                                         0, 32),
                                     ALU.add),
                                 inc="ve")
                        vb = b.cnt["ve"]
                        vb_hist[st["pk"]] = vb
                        vb_prev = vb
                        b.wait("vector", "ve", vb)
                        st["vf1"] = b.op("vector",
                                         lambda e, c=cb: e.tensor_scalar(
                                             SCR[0:32, STG + 128 * c:
                                                 STG + 128 * (c + 1)],
                                             SCR[0:32, STG + 128 * c:
                                                 STG + 128 * (c + 1)],
                                             0.0, None, ALU.max),
                                         inc="ve")
                        vf1_prev = st["vf1"]
                        load_f1(st["pk"] + 2)
                        pk = st["pk"] + 1

                    def emit_tail(st):
                        nonlocal vT_prev, f2_idx
                        cb = st["cb"]
                        b.wait("tensor", "ve", st["vf1"])
                        b.wait("tensor", "ve", vT_prev)
                        for e4 in range(4):
                            b.op("tensor",
                                 lambda e, e4=e4, c=cb: e.matmul(
                                     P2[0:32, 128 * e4:128 * (e4 + 1)],
                                     SCR[0:128, STG + 128 * c + 32 * e4:
                                         STG + 128 * c + 32 * (e4 + 1)],
                                     IDENT[:, :], start=(e4 == 0),
                                     stop=(e4 == 3),
                                     is_transpose=True),
                                 inc="pe")
                        peT = b.cnt["pe"]
                        b.wait("vector", "pe", peT)
                        vT = b.op("vector",
                                  lambda e: e.tensor_copy(
                                      SCR[0:32, INT:INT + 512],
                                      P2[0:32, 0:512]),
                                  inc="ve")
                        vT_prev = vT
                        for e4 in range(4):
                            es = st["es0"] + e4
                            slot = f2_idx % 4
                            its2 = f2_idx % 2
                            iw = "iwa" if its2 == 0 else "iwb"
                            pb = 1024 * its2
                            b.wait("tensor", F2SEM[slot], f2_ld[slot])
                            if e4 == 0:
                                b.wait("tensor", "ve", vT)
                            if f2_idx >= 2:
                                b.wait("tensor", "ac", ac_hist[f2_idx - 2])
                            for bk in range(2):
                                b.op("tensor",
                                     lambda e, e4=e4, bk=bk, pb=pb,
                                     fo=F2S[slot]: e.matmul(
                                         P0[0:32, pb + 512 * bk:
                                            pb + 512 * (bk + 1)],
                                         SCR[0:128, INT + 128 * e4:
                                             INT + 128 * e4 + 32],
                                         scr(fo + 512 * bk,
                                             fo + 512 * (bk + 1), 0, 128),
                                         start=True, stop=True),
                                     inc="pe")
                            pef2 = b.cnt["pe"]
                            f2_rd[slot] = pef2
                            load_f2(f2_idx + 4)
                            b.wait("scalar", "pe", pef2)
                            b.wait("scalar", iw, gp_prev_it[its2])
                            acv = b.op("scalar",
                                       lambda e, its2=its2, pb=pb: e.activation(
                                           scr(ITS2 + 1024 * its2,
                                               ITS2 + 1024 * its2 + 1024,
                                               0, 32),
                                           P0[0:32, pb:pb + 1024], AF.Copy,
                                           scale=1.0),
                                       inc="ac")
                            ac_hist[f2_idx] = acv
                            b.wait("gpsimd", "ac", acv)
                            gp_prev_it[its2] = b.dma(
                                "gpsimd",
                                itb[64 * es + 32 * gh:
                                    64 * es + 32 * gh + 32, :],
                                scr(ITS2 + 1024 * its2,
                                    ITS2 + 1024 * its2 + 1024, 0, 32),
                                iw)
                            f2_idx += 1

                    prev_st = None
                    for c in range(16):
                        st = emit_f1(c)
                        if prev_st is not None:
                            emit_tail(prev_st)
                        emit_bias(st)
                        prev_st = st
                    emit_tail(prev_st)
            if DBG == "ffb2":
                barrier("sync")
                return None

            b.wait("sync", "iwa", b.cnt["iwa"])
            b.wait("sync", "iwb", b.cnt["iwb"])
            b.wait("sync", "pe", b.cnt["pe"])
            b.wait("sync", "ve", b.cnt["ve"])
            b.wait("gpsimd", "pe", b.cnt["pe"])
            b.wait("gpsimd", "ve", b.cnt["ve"])
            b.wait("scalar", "pe", b.cnt["pe"])
            b.wait("scalar", "ve", b.cnt["ve"])
            b.wait("tensor", "ve", b.cnt["ve"])

            RHSS = (RHS2, RHS2 + 4096, DS, DS + 4096)
            RSEM_S = ("rha", "rhb", "rhc", "rhd")
            RSEM_G = ("rga", "rgb", "rgc", "rgd")

            def load_rhs(t):
                slot = t % 4
                lasts = [0, 0]
                for a in range(4):
                    g = 4 * t + a
                    src = AP(itb.tensor, g * DM, [[NG * DM, 128], [1, DM]])
                    eng = "sync" if a % 2 == 0 else "gpsimd"
                    sem = RSEM_S[slot] if a % 2 == 0 else RSEM_G[slot]
                    lasts[a % 2] = b.dma(eng,
                                         SCR[:, RHSS[slot] + 1024 * a:
                                             RHSS[slot] + 1024 * (a + 1)],
                                         src, sem)
                return lasts

            def load_xt(t, slot):
                return b.dma("scalar",
                             SCR[:, XT + 1024 * slot:XT + 1024 * (slot + 1)],
                             src_hbm[128 * t:128 * (t + 1), :],
                             "xsa" if slot == 0 else "xsb")

            pre_rhs = [load_rhs(0), load_rhs(1), load_rhs(2)]
            pre_xt = [load_xt(0, 0)]
            pesc_hist = [0] * NT
            vres_hist = [0] * NT
            st_hist = [0] * NT
            st_last = None
            for t in range(NT):
                if t + 3 < NT:
                    if t >= 1:
                        b.wait("sync", "pe", pesc_hist[t - 1])
                        b.wait("gpsimd", "pe", pesc_hist[t - 1])
                    pre_rhs.append(load_rhs(t + 3))
                if t + 1 < NT:
                    if t >= 1:
                        b.wait("scalar", "so", st_hist[t - 1])
                    pre_xt.append(load_xt(t + 1, (t + 1) % 2))
                b.wait("tensor", RSEM_S[t % 4], pre_rhs[t][0])
                b.wait("tensor", RSEM_G[t % 4], pre_rhs[t][1])
                if t >= 1:
                    b.wait("tensor", "ve", vres_hist[t - 1])
                for bk in range(2):
                    for a in range(4):
                        b.op("tensor",
                             lambda e, t=t, a=a, bk=bk: e.matmul(
                                 P1[:, 512 * bk:512 * (bk + 1)],
                                 scr(PRMP + 512 * t + 128 * a,
                                     PRMP + 512 * t + 128 * (a + 1)),
                                 SCR[:, RHSS[t % 4] + 1024 * a + 512 * bk:
                                     RHSS[t % 4] + 1024 * a
                                     + 512 * (bk + 1)],
                                 start=(a == 0), stop=(a == 3)),
                             inc="pe")
                pesc_hist[t] = b.cnt["pe"]
                b.wait("vector", "pe", pesc_hist[t])
                b.wait("vector", "xsa" if t % 2 == 0 else "xsb", pre_xt[t])
                vres_hist[t] = b.op("vector",
                                    lambda e, t=t: e.scalar_tensor_tensor(
                                        SCR[:, XT + 1024 * (t % 2):
                                            XT + 1024 * (t % 2) + 1024],
                                        P1[:, 0:1024], 0.5,
                                        SCR[:, XT + 1024 * (t % 2):
                                            XT + 1024 * (t % 2) + 1024],
                                        ALU.mult, ALU.add),
                                    inc="ve")
                b.wait("sync", "ve", vres_hist[t])
                if t >= 1:
                    b.wait("sync", "so", st_hist[t - 1])
                st_last = b.dma("sync", dst_hbm[128 * t:128 * (t + 1), :],
                                SCR[:, XT + 1024 * (t % 2):
                                    XT + 1024 * (t % 2) + 1024], "so")
                st_hist[t] = st_last
            return st_last

        def emit_attn(src_hbm, dst_hbm, hfm_ve):
            QF = XA + 0
            KF = XA + 4096
            VS = XA + 8192
            ET = XA + 10304
            SQR0 = XA + 12352
            SQR1 = TAIL + 6144
            WSL = TAIL
            SPOS = TAIL
            AOT = XA + 14528
            AOV = XA + 14656
            PVT = XA + 14720

            def sqr_off(i):
                return SQR0 if i % 2 == 0 else SQR1

            b.wait("tensor", "ve", hfm_ve)
            b.wait("scalar", "ve", hfm_ve)
            b.wait("gpsimd", "ve", hfm_ve)

            gtt_ld = b.dma("sync", SCR[:, GTT:GTT + 4095], gttd[:, :], "ld")
            b.wait("tensor", "ld", gtt_ld)

            b.op("gpsimd",
                 lambda e: e.memset(AP(SCR[:, :].tensor, VS + 64,
                                       [[SCRW, 128], [130, 16]]), 1.0),
                 inc="gm", n=1)
            b.op("gpsimd",
                 lambda e: e.memset(AP(SCR[:, :].tensor, VS + 129,
                                       [[SCRW, 128], [130, 16]]), 1.0),
                 inc="gm", n=1)
            gp_ones = b.cnt["gm"]

            NIT = 32
            DEPTH = 3

            for p in range(NPAIR):
                b.wait("sync", "pe", b.cnt["pe"])
                wl = None
                for (wt, woff) in ((wqt, 0), (wkt, 1), (wvt, 2)):
                    src = AP(wt.tensor, 128 * p, [[DM, 128], [128 * DM, 8],
                                                  [1, 128]])
                    wl = b.dma("sync",
                               AP(SCR[:, :].tensor, WSL + 1024 * woff,
                                  [[SCRW, 128], [128, 8], [1, 128]]),
                               src, "sx")
                b.wait("tensor", "sx", wl)
                b.wait("tensor", "ac", b.cnt["ac"])
                b.wait("tensor", "ve", b.cnt["ve"])

                aq_nb = [0] * 4
                for (dst, wof, bof) in ((QF, 0, BQ), (KF, 1, BK)):
                    for hh in range(2):
                        for nb in range(4):
                            b.wait("tensor", "ac", aq_nb[nb])
                            for m in range(8):
                                b.op("tensor",
                                     lambda e, wof=wof, hh=hh, nb=nb, m=m: e.matmul(
                                         P0[0:64, 512 * nb:512 * (nb + 1)],
                                         scr(WSL + 1024 * wof + 128 * m + 64 * hh,
                                             WSL + 1024 * wof + 128 * m
                                             + 64 * (hh + 1)),
                                         scr(HBASE + 2048 * m + 512 * nb,
                                             HBASE + 2048 * m + 512 * (nb + 1)),
                                         start=(m == 0), stop=(m == 7)),
                                     inc="pe")
                            peq = b.cnt["pe"]
                            b.wait("scalar", "pe", peq)
                            aq_nb[nb] = b.op(
                                "scalar",
                                lambda e, dst=dst, bof=bof, p=p, hh=hh,
                                nb=nb: e.activation(
                                    scr(dst + 2048 * hh + 512 * nb,
                                        dst + 2048 * hh + 512 * (nb + 1),
                                        0, 64),
                                    P0[0:64, 512 * nb:512 * (nb + 1)],
                                    AF.Identity,
                                    bias=scr(bof + p, bof + p + 1,
                                             64 * hh, 64 * (hh + 1)),
                                    scale=1.0),
                                inc="ac")
                b.wait("tensor", "ac", b.cnt["ac"])
                b.wait("tensor", "gm", gp_ones)
                vev_hist = [0] * NT
                for c in range(NT):
                    if c >= 2:
                        b.wait("tensor", "ve", vev_hist[c - 2])
                    for m in range(8):
                        b.op("tensor",
                             lambda e, c=c, m=m: e.matmul(
                                 P1[:, 512 * (c % 2):512 * (c % 2) + 128],
                                 scr(HBASE + 2048 * m + 128 * c,
                                     HBASE + 2048 * m + 128 * (c + 1)),
                                 scr(WSL + 2048 + 128 * m,
                                     WSL + 2048 + 128 * (m + 1)),
                                 start=(m == 0), stop=(m == 7)),
                             inc="pe")
                    pev = b.cnt["pe"]
                    b.wait("vector", "pe", pev)
                    for hh in range(2):
                        vev_hist[c] = b.op("vector",
                                   lambda e, c=c, hh=hh, p=p: e.tensor_tensor(
                                       SCR[:, VS + 130 * c + 65 * hh:
                                           VS + 130 * c + 65 * hh + 64],
                                       P1[:, 512 * (c % 2) + 64 * hh:
                                          512 * (c % 2) + 64 * hh + 64],
                                       SCR[:, BVR + 128 * p + 64 * hh:
                                           BVR + 128 * p + 64 * hh + 64],
                                       ALU.add),
                                   inc="ve")
                v_done = vev_hist[NT - 1]
                if p == NPAIR - 1:
                    b.wait("sync", "pe", b.cnt["pe"])
                    for m_ in range(8):
                        wot_ld = b.dma(
                            "sync",
                            SCR[:, HBASE + 1024 * m_:HBASE + 1024 * (m_ + 1)],
                            wot[128 * m_:128 * (m_ + 1), :], "sx")

                pe_qr = [0] * NIT
                ac_ev = [0] * NIT
                sy_wr = [0] * NIT
                gp_rd = [0] * NIT
                ac_exp = [0] * NIT
                ve_nrm = [0] * NIT
                ve_aot = [0] * NIT
                pe_pv = [0] * NIT
                pv_ve = [0] * NIT
                ve_cp = [[0, 0] for _ in range(NIT)]
                pe_half = [[0, 0] for _ in range(NIT)]

                def it_hq(i):
                    return (i % 2, i // 2)

                def emit_qr(i):
                    h, q = it_hq(i)
                    w0 = GTT + 1920 - 128 * q
                    if i >= 1:
                        b.wait("tensor", "ac", ac_ev[i - 1])
                    offs = [1024, 1536, 2048, 0, 512]
                    wds = [512, 512, 127, 512, 512]
                    for ci, (o, wd) in enumerate(zip(offs, wds)):
                        if ci == 3 and i >= DEPTH:
                            b.wait("tensor", "ve", ve_cp[i - DEPTH][1])
                        b.op("tensor",
                             lambda e, h=h, q=q, o=o, wd=wd, w0=w0: e.matmul(
                                 P0[:, o:o + wd],
                                 SCR[0:64, QF + 2048 * h + 128 * q:
                                     QF + 2048 * h + 128 * (q + 1)],
                                 SCR[0:64, w0 + o:w0 + o + wd],
                                 start=True, stop=True),
                             inc="pe")
                    pe_qr[i] = b.cnt["pe"]

                def emit_evac(i):
                    b.wait("scalar", "pe", pe_qr[i])
                    if i >= 2:
                        b.wait("scalar", ("qwa", "qwb", "qwc")[(i - 2) % 3],
                               sy_wr[i - 2])
                    ac_ev[i] = b.op("scalar",
                                    lambda e, i=i: e.activation(
                                        SCR[:, sqr_off(i):sqr_off(i) + QW],
                                        P0[:, 0:QW], AF.Copy, scale=1.0),
                                    inc="ac")

                def emit_wr(i):
                    b.wait("sync", "ac", ac_ev[i])
                    if i >= 3:
                        b.wait("sync", ("gra", "grb", "grc")[(i - 3) % 3],
                               gp_rd[i - 3])
                    sy_wr[i] = b.dma("sync",
                                     qsl[128 * (i % 3):128 * (i % 3) + 128, :],
                                     SCR[:, sqr_off(i):sqr_off(i) + QW],
                                     ("qwa", "qwb", "qwc")[i % 3])

                def emit_rd(i):
                    b.wait("gpsimd", ("qwa", "qwb", "qwc")[i % 3], sy_wr[i])
                    if i >= 3:
                        b.wait("gpsimd", "pe", pe_half[i - 3][1])
                    src = AP(qsl.tensor, (i % 3) * 128 * QW + 127,
                             [[QW - 1, 128], [1, 2048]])
                    gp_rd[i] = b.dma("gpsimd",
                                     SCR[:, SPOS + 2048 * (i % 3):
                                         SPOS + 2048 * (i % 3) + 2048],
                                     src, ("gra", "grb", "grc")[i % 3])

                def emit_half(i, hf):
                    h, q = it_hq(i)
                    PS = P1 if hf == 0 else P0
                    if hf == 0:
                        b.wait("tensor", ("gra", "grb", "grc")[i % 3],
                               gp_rd[i])
                    if hf == 0:
                        if i >= 1:
                            b.wait("tensor", "ve", pv_ve[i - 1])
                        else:
                            b.wait("tensor", "ve", v_done)
                    else:
                        b.wait("tensor", "ac", ac_ev[min(i + DEPTH - 1,
                                                         NIT - 1)])
                    jseq = (0, 4, 1, 5, 2, 6, 3, 7)
                    for j in jseq:
                        kc = 8 * hf + j
                        b.op("tensor",
                             lambda e, i=i, kc=kc, j=j, PS=PS: e.matmul(
                                 PS[:, 128 * j:128 * (j + 1)],
                                 SCR[:, SPOS + 2048 * (i % 3) + 128 * kc:
                                     SPOS + 2048 * (i % 3) + 128 * (kc + 1)],
                                 IDENT[:, :], start=(j == 0 or j == 4),
                                 stop=False,
                                 is_transpose=True),
                             inc="pe")
                    for j in jseq:
                        kc = 8 * hf + j
                        b.op("tensor",
                             lambda e, h=h, q=q, kc=kc, j=j, PS=PS: e.matmul(
                                 PS[:, 128 * j:128 * (j + 1)],
                                 SCR[0:64, KF + 2048 * h + 128 * kc:
                                     KF + 2048 * h + 128 * (kc + 1)],
                                 SCR[0:64, QF + 2048 * h + 128 * q:
                                     QF + 2048 * h + 128 * (q + 1)],
                                 start=False, stop=(j == 3 or j == 7)),
                             inc="pe")
                    pe_half[i][hf] = b.cnt["pe"]
                    b.wait("vector", "pe", pe_half[i][hf])
                    ve_cp[i][hf] = b.op("vector",
                                        lambda e, hf=hf, PS=PS: e.tensor_copy(
                                            SCR[:, ET + 1024 * hf:
                                                ET + 1024 * (hf + 1)],
                                            PS[:, 0:1024]),
                                        inc="ve")

                def emit_exp(i, hf):
                    b.wait("scalar", "ve", ve_cp[i][hf])
                    ac_exp[i] = b.op("scalar",
                                     lambda e, hf=hf: e.activation(
                                         SCR[:, ET + 1024 * hf:
                                             ET + 1024 * (hf + 1)],
                                         SCR[:, ET + 1024 * hf:
                                             ET + 1024 * (hf + 1)],
                                         AF.Exp, scale=0.125),
                                     inc="ac")

                def emit_pv(i):
                    h, q = it_hq(i)
                    b.wait("tensor", "ac", ac_exp[i])
                    if i >= 1:
                        b.wait("tensor", "ve", ve_aot[i - 1])
                    voff = 0 if h == 0 else 2
                    for kc in range(16):
                        b.op("tensor",
                             lambda e, voff=voff, kc=kc: e.matmul(
                                 P2[:, 0:128] if kc % 2 == 0 else P1[:, 0:128],
                                 SCR[:, ET + 128 * kc:ET + 128 * (kc + 1)],
                                 SCR[:, VS + 130 * kc + voff:
                                     VS + 130 * kc + voff + 128],
                                 start=(kc < 2), stop=(kc >= 14)),
                             inc="pe")
                    pe_pv[i] = b.cnt["pe"]

                def emit_norm(i):
                    h, q = it_hq(i)
                    pv0 = 0 if h == 0 else 63
                    zc = 64 if h == 0 else 127
                    b.wait("vector", "pe", pe_pv[i])
                    vc = b.op("vector",
                              lambda e: e.tensor_copy(scr(PVT, PVT + 128),
                                                      P1[:, 0:128]),
                              inc="ve")
                    pv_ve[i] = vc
                    b.wait("vector", "ve", vc)
                    v0 = b.op("vector",
                              lambda e: e.tensor_tensor(
                                  scr(PVT, PVT + 128),
                                  P2[:, 0:128], scr(PVT, PVT + 128), ALU.add),
                              inc="ve")
                    b.wait("vector", "ve", v0)
                    r1 = b.op("vector",
                              lambda e, zc=zc: e.reciprocal(
                                  scr(SMALL, SMALL + 1),
                                  scr(PVT + zc, PVT + zc + 1)),
                              inc="ve")
                    b.wait("vector", "ve", r1)
                    ve_nrm[i] = b.op("vector",
                                     lambda e, pv0=pv0: e.tensor_scalar(
                                         scr(AOV, AOV + 64),
                                         scr(PVT + pv0, PVT + pv0 + 64),
                                         scr(SMALL, SMALL + 1), None,
                                         ALU.mult),
                                     inc="ve")

                def emit_aot(i):
                    h, q = it_hq(i)
                    b.wait("tensor", "ve", ve_nrm[i])
                    b.wait("vector", "aow", b.cnt["aow"])
                    b.op("tensor",
                         lambda e: e.matmul(P2[0:64, 128:256],
                                            scr(AOV, AOV + 64),
                                            IDENT[:, :], start=True, stop=True,
                                            is_transpose=True),
                         inc="pe")
                    pea = b.cnt["pe"]
                    b.wait("vector", "pe", pea)
                    ve_aot[i] = b.op("vector",
                                     lambda e: e.tensor_copy(
                                         scr(AOT, AOT + 128, 0, 64),
                                         P2[0:64, 128:256]),
                                     inc="ve")
                    b.wait("gpsimd", "ve", ve_aot[i])
                    b.dma("gpsimd",
                          AP(aofm.tensor, (2 * p + h) * 64 * S + 128 * q,
                             [[S, 64], [1, 128]]),
                          scr(AOT, AOT + 128, 0, 64), "aow")

                for i in range(min(DEPTH, NIT)):
                    emit_qr(i); emit_evac(i); emit_wr(i); emit_rd(i)
                for i in range(NIT):
                    emit_half(i, 0)
                    emit_exp(i, 0)
                    if i >= 1:
                        emit_aot(i - 1)
                    emit_half(i, 1)
                    emit_exp(i, 1)
                    if i + DEPTH < NIT:
                        emit_qr(i + DEPTH); emit_evac(i + DEPTH)
                        emit_wr(i + DEPTH); emit_rd(i + DEPTH)
                    emit_pv(i)
                    emit_norm(i)
                emit_aot(NIT - 1)

            AOF = XA
            XTO = HBASE + 8192
            barrier("sync")
            barrier("gpsimd")
            lda = None
            for m in range(8):
                lda = b.dma("gpsimd",
                            SCR[:, AOF + 2048 * m:AOF + 2048 * (m + 1)],
                            aofm[128 * m:128 * (m + 1), :], "gra")
            b.wait("tensor", "sx", wot_ld)
            b.wait("tensor", "gra", lda)
            b.wait("tensor", "ac", b.cnt["ac"])
            st_last = None
            xt_pre = [b.dma("sync", SCR[:, XTO:XTO + 1024],
                            src_hbm[0:128, :], "xta")]
            v_hist = [0] * NT
            sto_hist = [0] * NT
            for t in range(NT):
                if t + 1 < NT:
                    if t >= 1:
                        b.wait("sync", "so", sto_hist[t - 1])
                    xt_pre.append(b.dma("sync",
                                        SCR[:, XTO + 1024 * ((t + 1) % 2):
                                            XTO + 1024 * ((t + 1) % 2) + 1024],
                                        src_hbm[128 * (t + 1):128 * (t + 2), :],
                                        "xta" if (t + 1) % 2 == 0 else "xtb"))
                if t >= 1:
                    b.wait("tensor", "ve", v_hist[t - 1])
                for nb in range(2):
                    for m in range(8):
                        b.op("tensor",
                             lambda e, t=t, nb=nb, m=m: e.matmul(
                                 P1[:, 512 * nb:512 * (nb + 1)],
                                 SCR[:, AOF + 2048 * m + 128 * t:
                                     AOF + 2048 * m + 128 * (t + 1)],
                                 SCR[:, HBASE + 1024 * m + 512 * nb:
                                     HBASE + 1024 * m + 512 * (nb + 1)],
                                 start=(m == 0), stop=(m == 7)),
                             inc="pe")
                peo = b.cnt["pe"]
                b.wait("vector", "pe", peo)
                b.wait("vector", "xta" if t % 2 == 0 else "xtb", xt_pre[t])
                vtmp = b.op("vector",
                     lambda e, t=t: e.tensor_tensor(
                         SCR[:, XTO + 1024 * (t % 2):XTO + 1024 * (t % 2) + 1024],
                         P1[:, 0:1024],
                         SCR[:, XTO + 1024 * (t % 2):XTO + 1024 * (t % 2) + 1024],
                         ALU.add),
                     inc="ve")
                b.wait("vector", "ve", vtmp)
                v_hist[t] = b.op("vector",
                                 lambda e, t=t: e.tensor_tensor(
                                     SCR[:, XTO + 1024 * (t % 2):
                                         XTO + 1024 * (t % 2) + 1024],
                                     SCR[:, XTO + 1024 * (t % 2):
                                         XTO + 1024 * (t % 2) + 1024],
                                     SCR[:, BOR:BOR + 1024], ALU.add),
                                 inc="ve")
                b.wait("sync", "ve", v_hist[t])
                st_last = b.dma("sync", dst_hbm[128 * t:128 * (t + 1), :],
                                SCR[:, XTO + 1024 * (t % 2):
                                    XTO + 1024 * (t % 2) + 1024], "so")
                sto_hist[t] = st_last
            return st_last

        for eng in ("vector", "scalar", "tensor", "gpsimd"):
            b.wait(eng, "ld", consts_ld)

        cur = x
        if DBG in ("ln", "lnfm"):
            hv = emit_ln(x, "tok" if DBG == "ln" else "fm")
            barrier("sync")
            for t in range(NT):
                b.wait("sync", "so", b.cnt["so"])
                b.dma("sync", out[128 * t:128 * (t + 1), :],
                      SCR[:, HBASE + 1024 * t:HBASE + 1024 * (t + 1)], "so")
            b.wait("sync", "so", b.cnt["so"])
            emit_block_end = True
        else:
            emit_block_end = False
        if not emit_block_end and DO_FFN1:
            hv = emit_ln(x, "tok")
            emit_ffn(x, xs1, hv)
            cur = xs1
        if not emit_block_end and DO_ATTN:
            hv = emit_ln(cur, "fm")
            emit_attn(cur, xs2 if DO_FFN2 else out, hv)
            cur = xs2 if DO_FFN2 else None
        if not emit_block_end and DO_FFN2:
            hv = emit_ln(cur if cur is not None else x, "tok")
            emit_ffn(cur, out, hv)

        if (not emit_block_end and DBG == "" and DO_FFN1
                and not DO_ATTN and not DO_FFN2):
            barrier("sync")
            for t in range(NT):
                b.wait("sync", "so", b.cnt["so"])
                lv = b.dma("sync", SCR[:, XA:XA + 1024],
                           xs1[128 * t:128 * (t + 1), :], "sx")
                b.wait("sync", "sx", lv)
                b.dma("sync", out[128 * t:128 * (t + 1), :],
                      SCR[:, XA:XA + 1024], "so")

        b.wait("sync", "so", b.cnt["so"])

        with nc.Block() as block:
            @block.sync
            def _(sync):
                for f in b.th["sync"]:
                    f(sync)

            @block.gpsimd
            def _(gpsimd):
                for f in b.th["gpsimd"]:
                    f(gpsimd)

            @block.tensor
            def _(tensor):
                for f in b.th["tensor"]:
                    f(tensor)

            @block.vector
            def _(vector):
                for f in b.th["vector"]:
                    f(vector)

            @block.scalar
            def _(scalar):
                for f in b.th["scalar"]:
                    f(scalar)

    return nc


def _host_prep(inputs):
    g = {}
    g["wqt"] = np.asarray(inputs["L0_attn_Wq"]).T
    g["wkt"] = np.asarray(inputs["L0_attn_Wk"]).T
    g["wvt"] = np.asarray(inputs["L0_attn_Wv"]).T
    g["wot"] = np.asarray(inputs["L0_attn_Wo"]).T
    g["bqs"] = np.asarray(inputs["L0_attn_bq"]).reshape(8, 128).T
    g["bks"] = np.asarray(inputs["L0_attn_bk"]).reshape(8, 128).T
    g["bvrep"] = np.broadcast_to(np.asarray(inputs["L0_attn_bv"]), (128, DM))
    g["borep"] = np.broadcast_to(np.asarray(inputs["L0_attn_bo"]), (128, DM))
    tbl = np.asarray(inputs["L0_attn_rel_table"])
    idx = np.clip(3070 - np.arange(4095), 0, 2046)
    gtt = tbl[idx].T
    g["gttd"] = np.concatenate([gtt, gtt], axis=0)
    ctrl = np.asarray(inputs["L0_ffn_controller"]).reshape(DM, NES)
    g["ctrlc"] = ctrl.reshape(8, 128, NES).transpose(1, 0, 2).reshape(128, 1024)
    g["f1r"] = np.asarray(inputs["L0_ffn_f1"]).reshape(DM, NES * FF)
    g["f1b"] = np.asarray(inputs["L0_ffn_bias"]).reshape(1, NES * FF)
    g["f1brep"] = np.broadcast_to(g["f1b"], (32, NES * FF))
    g["f2r"] = np.asarray(inputs["L0_ffn_f2"]).reshape(NES * FF, DM)
    g["identh"] = np.eye(128, dtype=np.float32)
    tie = np.linspace(0.0, 1e-6, TG, dtype=np.float32)
    g["tie4h"] = np.broadcast_to(np.tile(tie, 4), (128, 128))
    for k in g:
        g[k] = np.ascontiguousarray(g[k], dtype=np.float32)
    return g


def kernel(**inputs):
    xfull = np.ascontiguousarray(np.asarray(inputs["L0_attn_x"], dtype=np.float32))
    shards = [np.ascontiguousarray(xfull[i]) for i in range(8)]
    shared = _host_prep(inputs)
    nc = build_nc()
    in_maps = [dict(shared, x=shards[i]) for i in range(8)]
    res = run_bass_kernel_spmd(nc, in_maps, core_ids=list(range(8)))
    outs = [np.asarray(r["out"]) for r in res.results]
    return np.stack(outs, axis=0).astype(np.float32)
